# version 05
# baseline (speedup 1.0000x reference)
import math
import numpy as np
import ml_dtypes
import concourse.bass as bass
import concourse.mybir as mybir
from concourse.bass_utils import run_bass_kernel_spmd

F32 = mybir.dt.float32
BF16 = mybir.dt.bfloat16
AF = mybir.ActivationFunctionType
ALU = mybir.AluOpType

B, T, D = 4, 4096, 4096
HKV, H, DH = 8, 32, 128
RANK = 1024
KCH = D // 128
RCH = RANK // 128
NTC = T // 512
NSLOT = 4
CHUNKS = [[0, 3, 4, 7], [1, 2, 5, 6]]
PADC = [8, 16, 24, 32]
SCALE = 1.0 / math.sqrt(128.0)
INV_D = 1.0 / 4096.0
EPS = 1e-5

ENGINES = ("tensor", "scalar", "vector", "gpsimd", "sync")
SEMS = ("dW", "dY", "dK", "dO", "cT", "cS", "cV", "cP")


class Bld:
    def __init__(self):
        self.ops = {e: [] for e in ENGINES}
        self.cnt = {s: 0 for s in SEMS}
        self.lw = {e: {s: 0 for s in SEMS} for e in ENGINES}

    def emit(self, eng, fn, waits=(), inc=None, n=1):
        ws = []
        for t in waits:
            if t is None:
                continue
            s, v = t
            if self.lw[eng][s] < v:
                self.lw[eng][s] = v
                ws.append((s, v))
        tok = None
        if inc is not None:
            self.cnt[inc] += n
            tok = (inc, self.cnt[inc])
        self.ops[eng].append((ws, fn, inc, n))
        return tok


def build_nc(dbg=False):
    nc = bass.Bass("TRN2", target_bir_lowering=False, debug=False)
    KIND_KV = "ExternalOutput" if dbg else "Internal"

    xtl = nc.dram_tensor("xtl", [NTC, 128, KCH, 512], F32, kind="ExternalInput").ap()
    xql = nc.dram_tensor("xql", [NSLOT, 128, KCH, 512], F32, kind="ExternalInput").ap()
    xqt = nc.dram_tensor("xqt", [D, NSLOT * 512], F32, kind="ExternalInput").ap()
    wq = nc.dram_tensor("wq", [H, 128, KCH, 128], F32, kind="ExternalInput").ap()
    wk = nc.dram_tensor("wk", [HKV, 128, KCH, 128], F32, kind="ExternalInput").ap()
    wv = nc.dram_tensor("wv", [2, 128, KCH, 512], F32, kind="ExternalInput").ap()
    wo = nc.dram_tensor("wo", [KCH, 128, KCH, 128], F32, kind="ExternalInput").ap()
    vf = nc.dram_tensor("vf", [RCH, 128, KCH, 128], F32, kind="ExternalInput").ap()
    ut = nc.dram_tensor("ut", [KCH, 128, RCH, 128], F32, kind="ExternalInput").ap()
    svt = nc.dram_tensor("svt", [128, RCH], F32, kind="ExternalInput").ap()
    ccin = nc.dram_tensor("ccin", [128, T], F32, kind="ExternalInput").ap()
    ssin = nc.dram_tensor("ssin", [128, T], F32, kind="ExternalInput").ap()
    ccq = nc.dram_tensor("ccq", [128, NSLOT * 512], F32, kind="ExternalInput").ap()
    ssq = nc.dram_tensor("ssq", [128, NSLOT * 512], F32, kind="ExternalInput").ap()
    maskin = nc.dram_tensor("maskin", [NSLOT, 8, 128, 512], BF16, kind="ExternalInput").ap()
    kd = nc.dram_tensor("kd", [HKV, 128, T], BF16, kind=KIND_KV).ap()
    vd = nc.dram_tensor("vd", [HKV, 128, 32, 128], BF16, kind=KIND_KV).ap()
    if dbg:
        dqh = nc.dram_tensor("dqh", [128, 512], BF16, kind="ExternalOutput").ap()
        dattn = nc.dram_tensor("dattn", [128, 16384], BF16, kind="ExternalOutput").ap()
        drec = nc.dram_tensor("drec", [1, 512], F32, kind="ExternalOutput").ap()
        drecb = nc.dram_tensor("drecb", [128, 512], F32, kind="ExternalOutput").ap()
        dres0 = nc.dram_tensor("dres0", [128, 512], F32, kind="ExternalOutput").ap()
        dmu = nc.dram_tensor("dmu", [1, 512], F32, kind="ExternalOutput").ap()
        drr = nc.dram_tensor("drr", [1, 512], F32, kind="ExternalOutput").ap()
        dy0 = nc.dram_tensor("dy0", [128, 512], F32, kind="ExternalOutput").ap()
        dh = nc.dram_tensor("dh", [128, 4096], BF16, kind="ExternalOutput").ap()
        dres2 = nc.dram_tensor("dres2", [128, 512], F32, kind="ExternalOutput").ap()
        drrb = nc.dram_tensor("drrb", [128, 512], F32, kind="ExternalOutput").ap()
        dmurb = nc.dram_tensor("dmurb", [128, 512], F32, kind="ExternalOutput").ap()
        dybf = nc.dram_tensor("dybf", [128, 16384], BF16, kind="ExternalOutput").ap()
        dresall = nc.dram_tensor("dresall", [128, 16384], F32, kind="ExternalOutput").ap()
        ds2m = nc.dram_tensor("ds2m", [1, 512], F32, kind="ExternalOutput").ap()
        dvar = nc.dram_tensor("dvar", [1, 512], F32, kind="ExternalOutput").ap()
        dlnv = nc.dram_tensor("dlnv", [1, 512], F32, kind="ExternalOutput").ap()
        drr2 = nc.dram_tensor("drr2", [1, 512], F32, kind="ExternalOutput").ap()
    zt = nc.dram_tensor("zt", [D, NSLOT * 512], F32, kind="ExternalOutput").ap()

    bld = Bld()
    E = bld.emit

    import contextlib
    stack = contextlib.ExitStack()
    S = lambda shape, dt, name: stack.enter_context(nc.sbuf_tensor(name, shape, dt))

    RES = S([128, 16384], F32, "RES")
    XW = S([128, 16384], BF16, "XW")
    ATTN = S([128, 16384], BF16, "ATTN")
    WB = [S([128, 4096], BF16, "WB0"), S([128, 4096], BF16, "WB1")]
    MH = S([128, 4096], BF16, "MH")
    MASKS = MH
    HSB = MH
    KOUT = [S([128, 512], BF16, "KOUT0"), S([128, 512], BF16, "KOUT1")]
    QHB = KOUT
    CCQS = S([128, 512], F32, "CCQS")
    SSQS = S([128, 512], F32, "SSQS")
    XQR = [S([128, 512], F32, "XQR0"), S([128, 512], F32, "XQR1")]
    VOUT = [XQR[0][:, 0:256].bitcast(BF16), XQR[1][:, 0:256].bitcast(BF16)]
    SCA = [S([128, 512], F32, "SCA0"), S([128, 512], F32, "SCA1")]
    SCB = [S([128, 512], F32, "SCB0"), S([128, 512], F32, "SCB1")]
    SCC = [S([128, 512], F32, "SCC0"), S([128, 512], F32, "SCC1")]
    RECB = S([128, 512], F32, "RECB")
    RRB = S([128, 512], F32, "RRB")
    MURB = S([128, 512], F32, "MURB")
    LNT = SCA[0]
    SQBF = [S([128, 512], BF16, "SQBF0"), S([128, 512], BF16, "SQBF1")]
    ST_REC = S([1, 512], F32, "ST_REC")
    ST_MU = S([1, 512], F32, "ST_MU")
    ST_VAR = S([1, 512], F32, "ST_VAR")
    ST_VAR2 = S([1, 512], F32, "ST_VAR2")
    ST_RR = S([1, 512], F32, "ST_RR")
    ST_MUR = S([1, 512], F32, "ST_MUR")
    ST_T = S([1, 512], F32, "ST_T")
    ONESB = S([128, 1], BF16, "ONESB")
    ONESF = S([128, 1], F32, "ONESF")
    ONESROW = S([1, 128], F32, "ONESROW")
    EPSV = S([128, 1], F32, "EPSV")
    SVS = S([128, RCH], F32, "SVS")

    cc_t = RES[:, 0:4096]
    ss_t = RES[:, 4096:8192]
    xbfA = RES[:, 8192:16384].bitcast(BF16)
    kbuf = [RES[:, 0:2048].bitcast(BF16), RES[:, 2048:4096].bitcast(BF16)]
    vbuf = [RES[:, 4096:6144].bitcast(BF16), RES[:, 6144:8192].bitcast(BF16)]
    ptp = [RES[:, 8192:8704].bitcast(BF16), RES[:, 8704:9216].bitcast(BF16)]

    def res_m(m):
        return RES[:, 512 * m:512 * (m + 1)]

    P = lambda shape, name: stack.enter_context(nc.psum_tensor(name, shape, F32))
    pQ = [P([128, 512], "pQ0"), P([128, 512], "pQ1")]
    pSCa = P([128, 1024], "pSCa")
    pSCb = P([128, 1024], "pSCb")
    pAV = P([128, 512], "pAV")
    pSUM = P([128, 512], "pSUM")

    sems = {s: stack.enter_context(nc.semaphore(s)) for s in SEMS}

    def g_dma(dst, src, waits=()):
        return E("gpsimd", lambda e, d=dst, s2=src: e.dma_start(out=d, in_=s2),
                 waits=waits, inc="dW", n=16)

    def y_dma(dst, src, waits=(), sem="dY"):
        return E("sync", lambda e, d=dst, s2=src: e.dma_start(out=d, in_=s2),
                 waits=waits, inc=sem, n=16)

    def mm(dst, lh, rh, start, stop, waits=(), inc=None):
        return E("tensor",
                 lambda e, d=dst, l=lh, r=rh, st=start, sp=stop:
                 e.matmul(d, l, r, start=st, stop=sp, skip_group_check=True),
                 waits=waits, inc=inc)

    E("gpsimd", lambda e: e.memset(ONESB[:, :], 1.0))
    E("gpsimd", lambda e: e.memset(ONESF[:, :], 1.0))
    E("gpsimd", lambda e: e.memset(ONESROW[:, :], 1.0))
    init_tok = E("gpsimd", lambda e: e.memset(EPSV[:, :], EPS), inc="cP")
    cc_tok = y_dma(cc_t, ccin)
    ss_tok = y_dma(ss_t, ssin)
    sv_tok = y_dma(SVS[:, :], svt)

    wb_free = [None, None]
    pQ_free = [None, None]
    kout_free = [None, None]
    vout_free = [None, None]
    scA_free = [None, None]
    scB_free = [None, None]
    scC_free = [None, None]
    qh_rope = [None, None]
    qh_free = [None, None]
    xbfA_free = None
    xw_free = None

    for Ti in range(NTC):
        xbfA_tok = g_dma(xbfA.rearrange("p (c t) -> p c t", c=KCH), xtl[Ti],
                         waits=(xbfA_free, cc_tok, ss_tok))
        for g in range(HKV):
            i = g % 2
            wtok = g_dma(WB[i][:].rearrange("p (c m) -> p c m", c=KCH), wk[g],
                         waits=(wb_free[i],))
            wv_ = WB[i][:].rearrange("p (c m) -> p c m", c=KCH)
            xv_ = xbfA.rearrange("p (c t) -> p c t", c=KCH)
            t = None
            for c in range(KCH):
                t = mm(pQ[i][:, :], wv_[:, c, :], xv_[:, c, :], c == 0, c == KCH - 1,
                       waits=(wtok, xbfA_tok, pQ_free[i]),
                       inc=("cT" if c == KCH - 1 else None))
            wb_free[i] = t
            kmm = t
            ccs = cc_t[:, 512 * Ti:512 * (Ti + 1)]
            sss = ss_t[:, 512 * Ti:512 * (Ti + 1)]
            E("vector", lambda e, a=SCB[i][:, :], x=pQ[i][:, :], c2=ccs:
              e.tensor_tensor(a, x, c2, ALU.mult), waits=(kmm, scB_free[i], cc_tok, ss_tok))
            E("vector", lambda e, a=SCC[i][0:64, :], x=pQ[i][64:128, :], s2=sss[0:64, :]:
              e.tensor_tensor(a, x, s2, ALU.mult), waits=(scC_free[i],))
            r3 = E("vector", lambda e, a=SCC[i][64:128, :], x=pQ[i][0:64, :], s2=sss[64:128, :]:
                   e.tensor_tensor(a, x, s2, ALU.mult), inc="cV")
            pQ_free[i] = r3
            rp = E("gpsimd", lambda e, d=KOUT[i][:, :], a=SCB[i][:, :], b2=SCC[i][:, :]:
                   e.tensor_add(d, a, b2), waits=(r3, kout_free[i]), inc="cP")
            scB_free[i] = rp
            scC_free[i] = rp
            sp = y_dma(kd[g, :, 512 * Ti:512 * (Ti + 1)], KOUT[i][:, :], waits=(rp,), sem="dK")
            kout_free[i] = sp
        for hv in range(2):
            wvh_tok = g_dma(XW[:].rearrange("p (c n) -> p c n", c=KCH), wv[hv],
                            waits=(xw_free,))
            xwv = XW[:].rearrange("p (c n) -> p c n", c=KCH)
            xv_ = xbfA.rearrange("p (c t) -> p c t", c=KCH)
            for tt in range(4):
                i = tt % 2
                t = None
                for c in range(KCH):
                    t = mm(pQ[i][:, :], xv_[:, c, 128 * tt:128 * (tt + 1)], xwv[:, c, :],
                           c == 0, c == KCH - 1,
                           waits=(wvh_tok, xbfA_tok, pQ_free[i]),
                           inc=("cT" if c == KCH - 1 else None))
                xw_free = t
                ev = E("vector", lambda e, d=VOUT[i][:, :], s2=pQ[i][:, :]: e.tensor_copy(d, s2),
                       waits=(t, vout_free[i]), inc="cV")
                pQ_free[i] = ev
                last = None
                for j in range(4):
                    gg = 4 * hv + j
                    last = y_dma(vd[gg, :, 4 * Ti + tt, :],
                                 VOUT[i][:, 128 * j:128 * (j + 1)], waits=(ev,), sem="dK")
                vout_free[i] = last
        xbfA_free = xw_free

    spill_done = ("dK", bld.cnt["dK"])

    qh_free = [kout_free[0], kout_free[1]]
    xq_free = [vout_free[0], vout_free[1]]

    kb_tok = [None, None]
    vb_tok = [None, None]
    kb_free = [None, None]
    vb_free = [None, None]
    pt_free = [None, None]
    pSC_free = [None, None]
    pAV_free = init_tok
    pSUM_free = init_tok
    recB_free = None
    strec_free = init_tok
    masks_tok = None
    masks_free = None
    qrope_tok = None
    qrope_free = None
    attn_free = None
    xq_free = [None, None]
    sqbf_free = [None, None]
    lnt_free = None
    rrb_free = None
    out_toks = {}
    attn_done = None
    wq_toks = [None, None]
    xbfB_tok = None

    pSC = [pSCa, pSCb]

    def pt_pair_view(i):
        return ptp[i]

    for s in range(NSLOT):
        c = PADC[s]
        npair = c // 2
        masks_tok = y_dma(MASKS[:].rearrange("p (j n) -> p j n", j=8),
                          maskin[s].rearrange("j p n -> p j n"),
                          waits=(masks_free,))
        qrope_tok1 = y_dma(CCQS[:, :], ccq[:, 512 * s:512 * (s + 1)], waits=(qrope_free,))
        qrope_tok = y_dma(SSQS[:, :], ssq[:, 512 * s:512 * (s + 1)], waits=(qrope_free,))
        xbfB_tok = g_dma(XW[:].rearrange("p (c2 t) -> p c2 t", c2=KCH), xql[s],
                         waits=(xw_free,))
        xbf_v = XW[:].rearrange("p (c2 t) -> p c2 t", c2=KCH)

        def emit_qproj(h):
            i = h % 2
            wtok = g_dma(WB[i][:].rearrange("p (c2 m) -> p c2 m", c2=KCH), wq[h],
                         waits=(wb_free[i],))
            wq_toks[i] = wtok
            wv_ = WB[i][:].rearrange("p (c2 m) -> p c2 m", c2=KCH)
            t = None
            for c2 in range(KCH):
                t = mm(pQ[i][:, :], wv_[:, c2, :], xbf_v[:, c2, :], c2 == 0, c2 == KCH - 1,
                       waits=(wtok, xbfB_tok, pQ_free[i]),
                       inc=("cT" if c2 == KCH - 1 else None))
            wb_free[i] = t
            E("vector", lambda e, a=SCB[i][:, :], x=pQ[i][:, :]:
              e.tensor_tensor(a, x, CCQS[:, :], ALU.mult),
              waits=(t, scB_free[i], qrope_tok, qrope_tok1))
            E("vector", lambda e, a=SCC[i][0:64, :], x=pQ[i][64:128, :]:
              e.tensor_tensor(a, x, SSQS[0:64, :], ALU.mult), waits=(scC_free[i],))
            r3 = E("vector", lambda e, a=SCC[i][64:128, :], x=pQ[i][0:64, :]:
                   e.tensor_tensor(a, x, SSQS[64:128, :], ALU.mult), inc="cV")
            pQ_free[i] = r3
            rp = E("gpsimd", lambda e, d=QHB[i][:, :], a=SCB[i][:, :], b2=SCC[i][:, :]:
                   e.tensor_add(d, a, b2), waits=(r3, qh_free[i]), inc="cP")
            scB_free[i] = rp
            scC_free[i] = rp
            qh_rope[i] = rp

        emit_qproj(0)
        dbg_attn_tok = None
        if dbg and s == 3:
            y_dma(dqh, QHB[0][:, :], waits=(qh_rope[0],))

        pend = [None]

        def emit_tail(ph, av_tok, rc_tok):
            nonlocal recB_free, pAV_free, pSUM_free, attn_done, strec_free
            bc = mm(pSUM[:, :], ONESROW[:, :], ST_REC[0:1, :], True, True,
                    waits=(rc_tok, init_tok), inc="cT")
            strec_free = bc
            evb = E("vector", lambda e: e.tensor_copy(RECB[:, :], pSUM[:, :]),
                    waits=(bc, recB_free), inc="cV")
            pSUM_free = evb
            nr = E("vector", lambda e, d=ATTN[:, 512 * ph:512 * (ph + 1)]:
                   e.tensor_tensor(d, pAV[:, :], RECB[:, :], ALU.mult),
                   waits=(evb, av_tok, attn_free), inc="cV")
            recB_free = nr
            pAV_free = nr
            attn_done = nr
            if dbg and s == 3 and ph == 0:
                y_dma(drecb, RECB[:, :], waits=(evb,))

        for g in range(HKV):
            gi = g % 2
            if g == 0:
                kb_tok[gi] = y_dma(kbuf[gi][:, 0:128 * c], kd[g, :, 0:128 * c],
                                   waits=(spill_done, kb_free[gi], init_tok))
                vb_tok[gi] = y_dma(vbuf[gi][:, 0:128 * c].rearrange("p (c2 d) -> p c2 d", c2=c),
                                   vd[g, :, 0:c, :],
                                   waits=(vb_free[gi],))
            for j in range(4):
                h = 4 * g + j
                i = h % 2
                if h + 1 < H:
                    emit_qproj(h + 1)
                if pend[0] is not None:
                    emit_tail(*pend[0])
                    pend[0] = None
                vbv = vbuf[gi][:, 0:128 * c].rearrange("p (c2 d) -> p c2 d", c2=c)
                sc_tok = None
                use_tok = [None, None]
                av_sum_tok = None
                for p in range(npair):
                    pi = p % 2
                    for q in (0, 1):
                        kt = 2 * p + q
                        sc_tok = mm(pSC[pi][:, 512 * q:512 * (q + 1)],
                                    kbuf[gi][:, 128 * kt:128 * (kt + 1)], QHB[i][:, :],
                                    True, True,
                                    waits=(kb_tok[gi], qh_rope[i], pSC_free[pi]),
                                    inc=("cT" if q == 1 else None))
                    ex = E("scalar", lambda e, d=pt_pair_view(pi), s2=pSC[pi][:, :]:
                           e.activation(d, s2, AF.Exp, scale=SCALE),
                           waits=(sc_tok, pt_free[pi]), inc="cS")
                    pSC_free[pi] = ex
                    if p >= npair - 4:
                        jj = 2 * (p - (npair - 4))
                        mk = MASKS[:].rearrange("p (j n) -> p j n", j=8)
                        mview = MASKS[:, 512 * jj:512 * (jj + 2)]
                        ex = E("gpsimd", lambda e, d=pt_pair_view(pi), m2=mview:
                               e.tensor_mul(d, d, m2), waits=(ex, masks_tok), inc="cP")
                    ptv = pt_pair_view(pi)
                    t2 = None
                    for q in (0, 1):
                        kt = 2 * p + q
                        mm(pAV[:, :], vbv[:, kt, :], ptv[:, 512 * q:512 * (q + 1)],
                           kt == 0, kt == c - 1,
                           waits=(ex, vb_tok[gi], (pAV_free if kt == 0 else None)))
                        t2 = mm(pSUM[0:1, :], ONESB[:, :], ptv[:, 512 * q:512 * (q + 1)],
                                kt == 0, kt == c - 1,
                                waits=((pSUM_free if kt == 0 else None),),
                                inc=("cT" if q == 1 else None))
                    pt_free[pi] = t2
                    av_sum_tok = t2
                kb_free[gi] = sc_tok if j == 3 else kb_free[gi]
                vb_free[gi] = av_sum_tok if j == 3 else vb_free[gi]
                qh_free[i] = sc_tok
                if j == 3 and g + 1 < HKV:
                    ni = (g + 1) % 2
                    kb_tok[ni] = y_dma(kbuf[ni][:, 0:128 * c], kd[g + 1, :, 0:128 * c],
                                       waits=(kb_free[ni],))
                    vb_tok[ni] = y_dma(vbuf[ni][:, 0:128 * c].rearrange("p (c2 d) -> p c2 d", c2=c),
                                       vd[g + 1, :, 0:c, :],
                                       waits=(vb_free[ni],))
                rc = E("vector", lambda e: e.reciprocal(ST_REC[:, :], pSUM[0:1, :]),
                       waits=(av_sum_tok, strec_free), inc="cV")
                if dbg and s == 3 and h == 0:
                    y_dma(drec, ST_REC[:, :], waits=(rc,))
                pend[0] = (h, av_sum_tok, rc)
        emit_tail(*pend[0])
        pend[0] = None
        xw_free = wb_free[1]
        if dbg and s == 3:
            dbg_attn_tok = y_dma(dattn, ATTN[:, :], waits=(attn_done,))

        attn_v = ATTN[:].rearrange("p (c2 t) -> p c2 t", c2=KCH)
        mu_ap = pSCa[0:1, 0:512]
        s2_ap = pSCa[0:1, 512:1024]
        last_exp_tok = pSC_free[0]
        last_exp_tok2 = pSC_free[1]
        res_tok = [None] * KCH

        def emit_stats(pm, pra, psq, stop):
            mm(mu_ap, ONESF[:, :], res_m(pm), pm == 0, stop,
               waits=(pra, init_tok, (last_exp_tok if pm == 0 else None),
                      (last_exp_tok2 if pm == 0 else None)))
            t3 = mm(s2_ap, ONESB[:, :], SQBF[pm % 2][:, :], pm == 0, stop,
                    waits=(psq,), inc="cT")
            sqbf_free[pm % 2] = t3
            return t3

        prev_stat = None
        for m in range(KCH):
            i = m % 2
            wtok = g_dma(WB[i][:].rearrange("p (c2 m2) -> p c2 m2", c2=KCH), wo[m],
                         waits=(wb_free[i],))
            xqtok = y_dma(XQR[i][:, :], xqt[128 * m:128 * (m + 1), 512 * s:512 * (s + 1)],
                          waits=(xq_free[i],))
            wv_ = WB[i][:].rearrange("p (c2 m2) -> p c2 m2", c2=KCH)
            t = None
            for c2 in range(KCH):
                t = mm(pQ[i][:, :], wv_[:, c2, :], attn_v[:, c2, :], c2 == 0, c2 == KCH - 1,
                       waits=(wtok, attn_done, pQ_free[i]),
                       inc=("cT" if c2 == KCH - 1 else None))
            wb_free[i] = t
            ra = E("vector", lambda e, d=res_m(m), s2=pQ[i][:, :], x=XQR[i][:, :]:
                   e.tensor_tensor(d, s2, x, ALU.add),
                   waits=(t, xqtok, attn_done), inc="cV")
            pQ_free[i] = ra
            xq_free[i] = ra
            res_tok[m] = ra
            sq = E("scalar", lambda e, d=SQBF[i][:, :], s2=res_m(m):
                   e.activation(d, s2, AF.Square), waits=(ra, sqbf_free[i]), inc="cS")
            if prev_stat is not None:
                emit_stats(prev_stat[0], prev_stat[1], prev_stat[2], False)
            prev_stat = (m, ra, sq)
        stats_tok = emit_stats(prev_stat[0], prev_stat[1], prev_stat[2], True)

        def ln_finalize(stats_tok, bcast_psum_free1, bcast_psum_free2):
            a1 = E("vector", lambda e: e.tensor_scalar(ST_MU[:, :], mu_ap, INV_D, None, ALU.mult),
                   waits=(stats_tok,), inc="cV")
            a2 = E("vector", lambda e: e.tensor_scalar(ST_VAR[:, :], s2_ap, INV_D, None, ALU.mult),
                   waits=(stats_tok,), inc="cV")
            a3 = E("vector", lambda e: e.tensor_tensor(ST_T[:, :], ST_MU[:, :], ST_MU[:, :], ALU.mult),
                   waits=(a1,), inc="cV")
            a4 = E("vector", lambda e: e.tensor_tensor(ST_VAR2[:, :], ST_VAR[:, :], ST_T[:, :], ALU.subtract),
                   waits=(a2, a3), inc="cV")
            a5 = E("scalar", lambda e: e.activation(ST_T[:, :], ST_VAR2[:, :], AF.Ln, bias=EPSV[0:1, 0:1]),
                   waits=(a4, init_tok), inc="cS")
            a6 = E("scalar", lambda e: e.activation(ST_RR[:, :], ST_T[:, :], AF.Exp, scale=-0.5),
                   waits=(a5,), inc="cS")
            a7 = E("vector", lambda e: e.tensor_tensor(ST_MUR[:, :], ST_MU[:, :], ST_RR[:, :], ALU.mult),
                   waits=(a6,), inc="cV")
            if dbg_ln_dump[0]:
                y_dma(dbg_ln_dump[1], ST_VAR[:, :], waits=(a4,))
                y_dma(dbg_ln_dump[2], ST_VAR2[:, :], waits=(a4,))
                y_dma(dbg_ln_dump[3], ST_T[:, :], waits=(a5,))
                y_dma(dbg_ln_dump[4], ST_RR[:, :], waits=(a6,))
                dbg_ln_dump[0] = False
            b1 = mm(pSCb[:, 0:512], ONESROW[:, :], ST_RR[0:1, :], True, True,
                    waits=(a6, bcast_psum_free1), inc="cT")
            b2 = mm(pSCb[:, 512:1024], ONESROW[:, :], ST_MUR[0:1, :], True, True,
                    waits=(a7, bcast_psum_free2), inc="cT")
            e1 = E("vector", lambda e: e.tensor_copy(RRB[:, :], pSCb[:, 0:512]),
                   waits=(b1, rrb_free), inc="cV")
            e2 = E("vector", lambda e: e.tensor_copy(MURB[:, :], pSCb[:, 512:1024]),
                   waits=(b2,), inc="cV")
            return a2, e1, e2

        dbg_res0_tok = dbg_y0_tok = None
        dbg_ln_dump = [False, None, None, None, None]
        if dbg and s == 3:
            dbg_res0_tok = y_dma(dres0, res_m(0), waits=(res_tok[0],))
            dbg_res0_tok = y_dma(dresall, RES[:, :], waits=(res_tok[KCH - 1], dbg_res0_tok))
            dbg_ln_dump = [True, ds2m, dvar, dlnv, drr2]
        a2_ln1, e1, e2 = ln_finalize(stats_tok, pSC_free[1], pSC_free[1])
        if dbg and s == 3:
            y_dma(dmu, ST_MU[:, :], waits=(e2,))
            y_dma(drr, ST_RR[:, :], waits=(e2,))
            d1 = y_dma(drrb, RRB[:, :], waits=(e1,))
            d2 = y_dma(dmurb, MURB[:, :], waits=(e2,))
            rrb_free = d2
        y_tok = [None] * KCH
        ybf_tok = [None] * KCH
        for m in range(KCH):
            t1 = E("vector", lambda e, s2=res_m(m): e.tensor_tensor(LNT[:, :], s2, RRB[:, :], ALU.mult),
                   waits=(e1, res_tok[m], lnt_free), inc="cV")
            t2 = E("vector", lambda e, d=res_m(m): e.tensor_tensor(d, LNT[:, :], MURB[:, :], ALU.subtract),
                   waits=(e2, t1, dbg_res0_tok), inc="cV")
            lnt_free = t2
            y_tok[m] = t2
            yb = E("gpsimd", lambda e, d=ATTN[:, 512 * m:512 * (m + 1)], s2=res_m(m):
                   e.tensor_copy(d, s2), waits=(t2, wb_free[0], wb_free[1], dbg_attn_tok), inc="cP")
            ybf_tok[m] = yb
        rrb_free = y_tok[KCH - 1]

        ybf_v = ATTN[:].rearrange("p (c2 t) -> p c2 t", c2=KCH)
        hs_v = HSB[:].rearrange("p (r t) -> p r t", r=RCH)
        hs_tok = [None] * RCH
        for r in range(RCH):
            i = r % 2
            wtok = g_dma(WB[i][:].rearrange("p (c2 m2) -> p c2 m2", c2=KCH), vf[r],
                         waits=(wb_free[i],))
            wv_ = WB[i][:].rearrange("p (c2 m2) -> p c2 m2", c2=KCH)
            t = None
            for c2 in range(KCH):
                t = mm(pQ[i][:, :], wv_[:, c2, :], ybf_v[:, c2, :], c2 == 0, c2 == KCH - 1,
                       waits=(wtok, ybf_tok[c2], pQ_free[i]),
                       inc=("cT" if c2 == KCH - 1 else None))
            wb_free[i] = t
            hv = E("vector", lambda e, d=hs_v[:, r, :], s2=pQ[i][:, :], sv=SVS[:, r:r + 1]:
                   e.tensor_scalar(d, s2, sv, None, ALU.mult),
                   waits=(t, sv_tok, hs_tok[r]), inc="cV")
            pQ_free[i] = hv
            hs_tok[r] = hv
        attn_free = t
        if dbg and s == 3:
            dbg_y0_tok = y_dma(dy0, res_m(0), waits=(y_tok[0],))
            y_dma(dh, HSB[:, :], waits=(hs_tok[RCH - 1],))
            y_dma(dybf, ATTN[:, :], waits=(ybf_tok[KCH - 1],))
        res2_tok = [None] * KCH

        def emit_stats2(pm, pra, psq, stop):
            mm(mu_ap, ONESF[:, :], res_m(pm), pm == 0, stop,
               waits=(pra, (a2_ln1 if pm == 0 else None)))
            t3 = mm(s2_ap, ONESB[:, :], SQBF[pm % 2][:, :], pm == 0, stop,
                    waits=(psq,), inc="cT")
            sqbf_free[pm % 2] = t3
            return t3

        prev_stat = None
        for m in range(KCH):
            i = m % 2
            wtok = g_dma(WB[i][:, 0:1024].rearrange("p (c2 m2) -> p c2 m2", c2=RCH), ut[m],
                         waits=(wb_free[i],))
            wv_ = WB[i][:, 0:1024].rearrange("p (c2 m2) -> p c2 m2", c2=RCH)
            t = None
            for c2 in range(RCH):
                t = mm(pQ[i][:, :], wv_[:, c2, :], hs_v[:, c2, :], c2 == 0, c2 == RCH - 1,
                       waits=(wtok, hs_tok[c2], pQ_free[i]),
                       inc=("cT" if c2 == RCH - 1 else None))
            wb_free[i] = t
            ra = E("vector", lambda e, d=res_m(m), s2=pQ[i][:, :]:
                   e.tensor_tensor(d, s2, d, ALU.add),
                   waits=(t, y_tok[m], ybf_tok[m],
                          (dbg_y0_tok if m == 0 else None)), inc="cV")
            pQ_free[i] = ra
            res2_tok[m] = ra
            sq = E("scalar", lambda e, d=SQBF[i][:, :], s2=res_m(m):
                   e.activation(d, s2, AF.Square), waits=(ra, sqbf_free[i]), inc="cS")
            if prev_stat is not None:
                emit_stats2(prev_stat[0], prev_stat[1], prev_stat[2], False)
            prev_stat = (m, ra, sq)
        stats_tok = emit_stats2(prev_stat[0], prev_stat[1], prev_stat[2], True)

        dbg_res2_tok = None
        if dbg and s == 3:
            dbg_res2_tok = y_dma(dres2, res_m(0), waits=(res2_tok[0],))
        a2_ln2, e1, e2 = ln_finalize(stats_tok, e1, e2)
        for m in range(KCH):
            t1 = E("vector", lambda e, s2=res_m(m): e.tensor_tensor(LNT[:, :], s2, RRB[:, :], ALU.mult),
                   waits=(e1, res2_tok[m], lnt_free), inc="cV")
            t2 = E("vector", lambda e, d=res_m(m): e.tensor_tensor(d, LNT[:, :], MURB[:, :], ALU.subtract),
                   waits=(e2, t1, (dbg_res2_tok if m == 0 else None)), inc="cV")
            lnt_free = t2
            ot = y_dma(zt[128 * m:128 * (m + 1), 512 * s:512 * (s + 1)], res_m(m),
                       waits=(t2,), sem="dO")
            out_toks[(s, m)] = ot
        rrb_free = t2

        if s + 1 < NSLOT:
            kb_free[0] = out_toks[(s, 3)]
            kb_free[1] = out_toks[(s, 3)]
            vb_free[0] = out_toks[(s, 15)]
            vb_free[1] = out_toks[(s, 15)]
            pt_free[0] = out_toks[(s, 17)]
            pt_free[1] = out_toks[(s, 17)]
            pSC_free[0] = a2_ln2
            pSC_free[1] = e2
            masks_free = stats_tok
            qrope_free = ("cP", bld.cnt["cP"])

    total_dO = bld.cnt["dO"]
    E("sync", lambda e: e.wait_ge(sems["dO"], total_dO))

    nops = {e: len(bld.ops[e]) for e in ENGINES}

    with nc.Block() as block:
        def make_runner(name):
            def run(e):
                for (ws, fn, inc, n) in bld.ops[name]:
                    for (s2, v) in ws:
                        e.wait_ge(sems[s2], v)
                    ins = fn(e)
                    if inc is not None:
                        ins.then_inc(sems[inc], n)
            return run
        block.tensor(make_runner("tensor"))
        block.scalar(make_runner("scalar"))
        block.vector(make_runner("vector"))
        block.gpsimd(make_runner("gpsimd"))
        block.sync(make_runner("sync"))

    stack.close()
    return nc, nops


def _rope_tables():
    inv = (1.0 / (10000.0 ** (np.arange(0, 64, dtype=np.float32) / np.float32(64.0)))).astype(np.float32)
    pos = np.arange(T, dtype=np.float32)
    ang = inv[:, None] * pos[None, :]
    cos = np.cos(ang).astype(np.float32)
    sin = np.sin(ang).astype(np.float32)
    cc = np.concatenate([cos, cos], axis=0)
    ss = np.concatenate([-sin, sin], axis=0)
    return cc, ss


def _masks_for_role(r):
    out = np.zeros((NSLOT, 8, 128, 512), dtype=np.float32)
    for s in range(NSLOT):
        J = CHUNKS[r][s]
        cpad = PADC[s]
        for jj in range(8):
            kt = cpad - 8 + jj
            if kt < 4 * J:
                out[s, jj] = 1.0
            elif kt < 4 * J + 4:
                q = 512 * J + np.arange(512)[None, :]
                k = 128 * kt + np.arange(128)[:, None]
                out[s, jj] = (q >= k).astype(np.float32)
    return out.astype(ml_dtypes.bfloat16)


def _chunked(w, nch):
    din, n = w.shape
    csz = n // nch
    return np.ascontiguousarray(
        w.reshape(din // 128, 128, nch, csz).transpose(2, 1, 0, 3))


def prep_in_maps(L0_attn_X, L0_attn_Wq, L0_attn_Wk, L0_attn_Wv, L0_attn_Wo,
                 L0_ffn_U, L0_ffn_S, L0_ffn_V):
    X = np.asarray(L0_attn_X, dtype=np.float32)
    wq = _chunked(np.asarray(L0_attn_Wq, dtype=np.float32), H)
    wk = _chunked(np.asarray(L0_attn_Wk, dtype=np.float32), HKV)
    wv = _chunked(np.asarray(L0_attn_Wv, dtype=np.float32), 2)
    wo = _chunked(np.asarray(L0_attn_Wo, dtype=np.float32), KCH)
    u = np.asarray(L0_ffn_U, dtype=np.float32)
    sv = np.asarray(L0_ffn_S, dtype=np.float32)
    vf = _chunked(np.asarray(L0_ffn_V, dtype=np.float32), RCH)
    ut = _chunked(np.ascontiguousarray(u.T), KCH)
    svt = np.ascontiguousarray(sv.reshape(RCH, 128).T)

    cc, ss = _rope_tables()
    masks = [_masks_for_role(0), _masks_for_role(1)]

    in_maps = []
    xts = [np.ascontiguousarray(X[b].T) for b in range(B)]
    xtls = [_chunked(xts[b], NTC) for b in range(B)]
    for cidx in range(8):
        b, r = cidx // 2, cidx % 2
        qcols = np.concatenate([np.arange(512 * J, 512 * (J + 1)) for J in CHUNKS[r]])
        xqt = np.ascontiguousarray(xts[b][:, qcols])
        in_maps.append({
            "xtl": xtls[b], "xql": _chunked(xqt, NSLOT), "xqt": xqt,
            "wq": wq, "wk": wk, "wv": wv, "wo": wo,
            "vf": vf, "ut": ut, "svt": svt,
            "ccin": cc, "ssin": ss,
            "ccq": np.ascontiguousarray(cc[:, qcols]),
            "ssq": np.ascontiguousarray(ss[:, qcols]),
            "maskin": masks[r],
        })
    return in_maps


def kernel(L0_attn_X, L0_attn_Wq, L0_attn_Wk, L0_attn_Wv, L0_attn_Wo,
           L0_ffn_U, L0_ffn_S, L0_ffn_V):
    in_maps = prep_in_maps(L0_attn_X, L0_attn_Wq, L0_attn_Wk, L0_attn_Wv,
                           L0_attn_Wo, L0_ffn_U, L0_ffn_S, L0_ffn_V)
    nc, _ = build_nc()
    res = run_bass_kernel_spmd(nc, in_maps, core_ids=list(range(8)))
    out = np.empty((B, T, D), dtype=np.float32)
    for cidx in range(8):
        ztr = res.results[cidx]["zt"]
        b, r = cidx // 2, cidx % 2
        for s2, J in enumerate(CHUNKS[r]):
            out[b, 512 * J:512 * (J + 1), :] = ztr[:, 512 * s2:512 * (s2 + 1)].T
    return out


if __name__ == "__main__":
    nc, nops = build_nc()
    print("instruction counts per engine:", nops)
    print("sbuf bytes remaining:", nc.sbuf_bytes_remaining)


# version 09
# speedup vs baseline: 1.1524x; 1.1524x over previous
import math
import numpy as np
import ml_dtypes
import concourse.bass as bass
import concourse.mybir as mybir
from concourse.bass_utils import run_bass_kernel_spmd

F32 = mybir.dt.float32
BF16 = mybir.dt.bfloat16
AF = mybir.ActivationFunctionType
ALU = mybir.AluOpType

B, T, D = 4, 4096, 4096
HKV, H, DH = 8, 32, 128
RANK = 1024
KCH = D // 128
RCH = RANK // 128
NTC = T // 512
NSLOT = 4
CHUNKS = [[0, 3, 4, 7], [1, 2, 5, 6]]
PADC = [8, 16, 24, 32]
SCALE = 1.0 / math.sqrt(128.0)
INV_D = 1.0 / 4096.0
EPS = 1e-5

ENGINES = ("tensor", "scalar", "vector", "gpsimd", "sync")
SEMS = ("dW", "dY", "dK", "dO", "cT", "cS", "cV", "cP")


class Bld:
    def __init__(self):
        self.ops = {e: [] for e in ENGINES}
        self.cnt = {s: 0 for s in SEMS}
        self.lw = {e: {s: 0 for s in SEMS} for e in ENGINES}

    def emit(self, eng, fn, waits=(), inc=None, n=1):
        ws = []
        for t in waits:
            if t is None:
                continue
            s, v = t
            if self.lw[eng][s] < v:
                self.lw[eng][s] = v
                ws.append((s, v))
        tok = None
        if inc is not None:
            self.cnt[inc] += n
            tok = (inc, self.cnt[inc])
        self.ops[eng].append((ws, fn, inc, n))
        return tok


def build_nc(dbg=False):
    nc = bass.Bass("TRN2", target_bir_lowering=False, debug=False)
    KIND_KV = "ExternalOutput" if dbg else "Internal"

    xtl = nc.dram_tensor("xtl", [NTC, 128, KCH, 512], F32, kind="ExternalInput").ap()
    xql = nc.dram_tensor("xql", [NSLOT, 128, KCH, 512], F32, kind="ExternalInput").ap()
    xqt = nc.dram_tensor("xqt", [D, NSLOT * 512], F32, kind="ExternalInput").ap()
    wq = nc.dram_tensor("wq", [H, 128, KCH, 128], F32, kind="ExternalInput").ap()
    wk = nc.dram_tensor("wk", [HKV, 128, KCH, 128], F32, kind="ExternalInput").ap()
    wv = nc.dram_tensor("wv", [2, 128, KCH, 512], F32, kind="ExternalInput").ap()
    wo = nc.dram_tensor("wo", [KCH, 128, KCH, 128], F32, kind="ExternalInput").ap()
    vf = nc.dram_tensor("vf", [RCH, 128, KCH, 128], F32, kind="ExternalInput").ap()
    ut = nc.dram_tensor("ut", [KCH, 128, RCH, 128], F32, kind="ExternalInput").ap()
    svt = nc.dram_tensor("svt", [128, RCH], F32, kind="ExternalInput").ap()
    ccin = nc.dram_tensor("ccin", [128, T], F32, kind="ExternalInput").ap()
    ssin = nc.dram_tensor("ssin", [128, T], F32, kind="ExternalInput").ap()
    ccq = nc.dram_tensor("ccq", [128, NSLOT * 512], F32, kind="ExternalInput").ap()
    ssq = nc.dram_tensor("ssq", [128, NSLOT * 512], F32, kind="ExternalInput").ap()
    maskin = nc.dram_tensor("maskin", [NSLOT, 8, 128, 512], BF16, kind="ExternalInput").ap()
    kd = nc.dram_tensor("kd", [HKV, 128, T], BF16, kind=KIND_KV).ap()
    vd = nc.dram_tensor("vd", [HKV, 128, 32, 128], BF16, kind=KIND_KV).ap()
    if dbg:
        dqh = nc.dram_tensor("dqh", [128, 512], BF16, kind="ExternalOutput").ap()
        dattn = nc.dram_tensor("dattn", [128, 16384], BF16, kind="ExternalOutput").ap()
        drec = nc.dram_tensor("drec", [1, 512], F32, kind="ExternalOutput").ap()
        drecb = nc.dram_tensor("drecb", [128, 512], F32, kind="ExternalOutput").ap()
        dres0 = nc.dram_tensor("dres0", [128, 512], F32, kind="ExternalOutput").ap()
        dmu = nc.dram_tensor("dmu", [1, 512], F32, kind="ExternalOutput").ap()
        drr = nc.dram_tensor("drr", [1, 512], F32, kind="ExternalOutput").ap()
        dy0 = nc.dram_tensor("dy0", [128, 512], F32, kind="ExternalOutput").ap()
        dh = nc.dram_tensor("dh", [128, 4096], BF16, kind="ExternalOutput").ap()
        dres2 = nc.dram_tensor("dres2", [128, 512], F32, kind="ExternalOutput").ap()
        drrb = nc.dram_tensor("drrb", [128, 512], F32, kind="ExternalOutput").ap()
        dmurb = nc.dram_tensor("dmurb", [128, 512], F32, kind="ExternalOutput").ap()
        dybf = nc.dram_tensor("dybf", [128, 16384], BF16, kind="ExternalOutput").ap()
        dresall = nc.dram_tensor("dresall", [128, 16384], F32, kind="ExternalOutput").ap()
        ds2m = nc.dram_tensor("ds2m", [1, 512], F32, kind="ExternalOutput").ap()
        dvar = nc.dram_tensor("dvar", [1, 512], F32, kind="ExternalOutput").ap()
        dlnv = nc.dram_tensor("dlnv", [1, 512], F32, kind="ExternalOutput").ap()
        drr2 = nc.dram_tensor("drr2", [1, 512], F32, kind="ExternalOutput").ap()
    zt = nc.dram_tensor("zt", [D, NSLOT * 512], F32, kind="ExternalOutput").ap()

    bld = Bld()
    E = bld.emit

    import contextlib
    stack = contextlib.ExitStack()
    S = lambda shape, dt, name: stack.enter_context(nc.sbuf_tensor(name, shape, dt))

    RES = S([128, 16384], F32, "RES")
    XW = S([128, 16384], BF16, "XW")
    ATTN = S([128, 16384], BF16, "ATTN")
    WB = [S([128, 4096], BF16, "WB0"), S([128, 4096], BF16, "WB1")]
    MH = S([128, 4096], BF16, "MH")
    MASKS = MH
    HSB = MH
    KOUT = [S([128, 512], BF16, "KOUT0"), S([128, 512], BF16, "KOUT1")]
    QHB = KOUT
    CCQS = S([128, 512], F32, "CCQS")
    SSQS = S([128, 512], F32, "SSQS")
    XQR = [S([128, 512], F32, "XQR0"), S([128, 512], F32, "XQR1")]
    VOUT = [XQR[0][:, 0:256].bitcast(BF16), XQR[1][:, 0:256].bitcast(BF16)]
    SCA = [S([128, 512], F32, "SCA0"), S([128, 512], F32, "SCA1")]
    SCB = [S([128, 512], F32, "SCB0"), S([128, 512], F32, "SCB1")]
    SCC = [S([128, 512], F32, "SCC0"), S([128, 512], F32, "SCC1")]
    RECB = S([128, 512], F32, "RECB")
    RRB = S([128, 512], F32, "RRB")
    MURB = S([128, 512], F32, "MURB")
    LNT = SCA[0]
    SQBF = [S([128, 512], BF16, "SQBF0"), S([128, 512], BF16, "SQBF1")]
    ST_REC = S([1, 512], F32, "ST_REC")
    ST_MU = S([1, 512], F32, "ST_MU")
    ST_VAR = S([1, 512], F32, "ST_VAR")
    ST_VAR2 = S([1, 512], F32, "ST_VAR2")
    ST_RR = S([1, 512], F32, "ST_RR")
    ST_MUR = S([1, 512], F32, "ST_MUR")
    ST_T = S([1, 512], F32, "ST_T")
    ONESB = S([128, 1], BF16, "ONESB")
    ONESF = S([128, 1], F32, "ONESF")
    ONESROW = S([1, 128], F32, "ONESROW")
    EPSV = S([128, 1], F32, "EPSV")
    SVS = S([128, RCH], F32, "SVS")

    cc_t = RES[:, 0:4096]
    ss_t = RES[:, 4096:8192]
    xbfA = RES[:, 8192:16384].bitcast(BF16)
    kbuf = [RES[:, 0:2048].bitcast(BF16), RES[:, 2048:4096].bitcast(BF16)]
    vbuf = [RES[:, 4096:6144].bitcast(BF16), RES[:, 6144:8192].bitcast(BF16)]
    ptp = [RES[:, 8192:8704].bitcast(BF16), RES[:, 8704:9216].bitcast(BF16)]

    def res_m(m):
        return RES[:, 512 * m:512 * (m + 1)]

    P = lambda shape, name: stack.enter_context(nc.psum_tensor(name, shape, F32))
    pQ = [P([128, 512], "pQ0"), P([128, 512], "pQ1")]
    pSCa = P([128, 1024], "pSCa")
    pSCb = P([128, 1024], "pSCb")
    pAV = P([128, 512], "pAV")
    pSUM = P([128, 512], "pSUM")

    sems = {s: stack.enter_context(nc.semaphore(s)) for s in SEMS}

    def g_dma(dst, src, waits=()):
        return E("gpsimd", lambda e, d=dst, s2=src: e.dma_start(out=d, in_=s2),
                 waits=waits, inc="dW", n=16)

    def y_dma(dst, src, waits=(), sem="dY"):
        return E("sync", lambda e, d=dst, s2=src: e.dma_start(out=d, in_=s2),
                 waits=waits, inc=sem, n=16)

    def mm(dst, lh, rh, start, stop, waits=(), inc=None):
        return E("tensor",
                 lambda e, d=dst, l=lh, r=rh, st=start, sp=stop:
                 e.matmul(d, l, r, start=st, stop=sp, skip_group_check=True),
                 waits=waits, inc=inc)

    E("gpsimd", lambda e: e.memset(ONESB[:, :], 1.0))
    E("gpsimd", lambda e: e.memset(ONESF[:, :], 1.0))
    E("gpsimd", lambda e: e.memset(ONESROW[:, :], 1.0))
    init_tok = E("gpsimd", lambda e: e.memset(EPSV[:, :], EPS), inc="cP")
    cc_tok = y_dma(cc_t, ccin)
    ss_tok = y_dma(ss_t, ssin)
    sv_tok = y_dma(SVS[:, :], svt)

    wb_free = [None, None]
    pQ_free = [None, None]
    kout_free = [None, None]
    vout_free = [None, None]
    scA_free = [None, None]
    scB_free = [None, None]
    scC_free = [None, None]
    qh_rope = [None, None]
    qh_free = [None, None]
    xbfA_free = None
    xw_free = None

    for Ti in range(NTC):
        xbfA_tok = g_dma(xbfA.rearrange("p (c t) -> p c t", c=KCH), xtl[Ti],
                         waits=(xbfA_free, cc_tok, ss_tok))
        for g in range(HKV):
            i = g % 2
            wtok = g_dma(WB[i][:].rearrange("p (c m) -> p c m", c=KCH), wk[g],
                         waits=(wb_free[i],))
            wv_ = WB[i][:].rearrange("p (c m) -> p c m", c=KCH)
            xv_ = xbfA.rearrange("p (c t) -> p c t", c=KCH)
            t = None
            for c in range(KCH):
                t = mm(pQ[i][:, :], wv_[:, c, :], xv_[:, c, :], c == 0, c == KCH - 1,
                       waits=(wtok, xbfA_tok, pQ_free[i]),
                       inc=("cT" if c == KCH - 1 else None))
            wb_free[i] = t
            kmm = t
            ccs = cc_t[:, 512 * Ti:512 * (Ti + 1)]
            sss = ss_t[:, 512 * Ti:512 * (Ti + 1)]
            E("vector", lambda e, a=SCB[i][:, :], x=pQ[i][:, :], c2=ccs:
              e.tensor_tensor(a, x, c2, ALU.mult), waits=(kmm, scB_free[i], cc_tok, ss_tok))
            E("vector", lambda e, a=SCC[i][0:64, :], x=pQ[i][64:128, :], s2=sss[0:64, :]:
              e.tensor_tensor(a, x, s2, ALU.mult), waits=(scC_free[i],))
            r3 = E("vector", lambda e, a=SCC[i][64:128, :], x=pQ[i][0:64, :], s2=sss[64:128, :]:
                   e.tensor_tensor(a, x, s2, ALU.mult), inc="cV")
            pQ_free[i] = r3
            rp = E("gpsimd", lambda e, d=KOUT[i][:, :], a=SCB[i][:, :], b2=SCC[i][:, :]:
                   e.tensor_add(d, a, b2), waits=(r3, kout_free[i]), inc="cP")
            scB_free[i] = rp
            scC_free[i] = rp
            sp = y_dma(kd[g, :, 512 * Ti:512 * (Ti + 1)], KOUT[i][:, :], waits=(rp,), sem="dK")
            kout_free[i] = sp
        for hv in range(2):
            wvh_tok = g_dma(XW[:].rearrange("p (c n) -> p c n", c=KCH), wv[hv],
                            waits=(xw_free,))
            xwv = XW[:].rearrange("p (c n) -> p c n", c=KCH)
            xv_ = xbfA.rearrange("p (c t) -> p c t", c=KCH)
            for tt in range(4):
                i = tt % 2
                t = None
                for c in range(KCH):
                    t = mm(pQ[i][:, :], xv_[:, c, 128 * tt:128 * (tt + 1)], xwv[:, c, :],
                           c == 0, c == KCH - 1,
                           waits=(wvh_tok, xbfA_tok, pQ_free[i]),
                           inc=("cT" if c == KCH - 1 else None))
                xw_free = t
                ev = E("vector", lambda e, d=VOUT[i][:, :], s2=pQ[i][:, :]: e.tensor_copy(d, s2),
                       waits=(t, vout_free[i]), inc="cV")
                pQ_free[i] = ev
                last = None
                for j in range(4):
                    gg = 4 * hv + j
                    last = y_dma(vd[gg, :, 4 * Ti + tt, :],
                                 VOUT[i][:, 128 * j:128 * (j + 1)], waits=(ev,), sem="dK")
                vout_free[i] = last
        xbfA_free = xw_free

    spill_done = ("dK", bld.cnt["dK"])

    qh_free = [kout_free[0], kout_free[1]]
    xq_free = [vout_free[0], vout_free[1]]

    kb_tok = [None, None]
    vb_tok = [None, None]
    kb_free = [None, None]
    vb_free = [None, None]
    pt_free = [None, None]
    pSC_free = [None, None]
    pAV_free = init_tok
    pSUM_free = init_tok
    recB_free = None
    strec_free = init_tok
    masks_tok = None
    masks_free = None
    qrope_tok = None
    qrope_free = None
    attn_free = None
    xq_free = [None, None]
    sqbf_free = [None, None]
    lnt_free = None
    rrb_free = None
    out_toks = {}
    attn_done = None
    wq_toks = [None, None]
    xbfB_tok = None

    pSC = [pSCa, pSCb]

    def pt_pair_view(i):
        return ptp[i]

    for s in range(NSLOT):
        c = PADC[s]
        npair = c // 2
        masks_tok = y_dma(MASKS[:].rearrange("p (j n) -> p j n", j=8),
                          maskin[s].rearrange("j p n -> p j n"),
                          waits=(masks_free,))
        qrope_tok1 = y_dma(CCQS[:, :], ccq[:, 512 * s:512 * (s + 1)], waits=(qrope_free,))
        qrope_tok = y_dma(SSQS[:, :], ssq[:, 512 * s:512 * (s + 1)], waits=(qrope_free,))
        xbfB_tok = g_dma(XW[:].rearrange("p (c2 t) -> p c2 t", c2=KCH), xql[s],
                         waits=(xw_free,))
        xbf_v = XW[:].rearrange("p (c2 t) -> p c2 t", c2=KCH)

        def emit_qproj(h):
            i = h % 2
            wtok = g_dma(WB[i][:].rearrange("p (c2 m) -> p c2 m", c2=KCH), wq[h],
                         waits=(wb_free[i],))
            wq_toks[i] = wtok
            wv_ = WB[i][:].rearrange("p (c2 m) -> p c2 m", c2=KCH)
            t = None
            for c2 in range(KCH):
                t = mm(pQ[i][:, :], wv_[:, c2, :], xbf_v[:, c2, :], c2 == 0, c2 == KCH - 1,
                       waits=(wtok, xbfB_tok, pQ_free[i]),
                       inc=("cT" if c2 == KCH - 1 else None))
            wb_free[i] = t
            E("vector", lambda e, a=SCB[i][:, :], x=pQ[i][:, :]:
              e.tensor_tensor(a, x, CCQS[:, :], ALU.mult),
              waits=(t, scB_free[i], qrope_tok, qrope_tok1))
            E("vector", lambda e, a=SCC[i][0:64, :], x=pQ[i][64:128, :]:
              e.tensor_tensor(a, x, SSQS[0:64, :], ALU.mult), waits=(scC_free[i],))
            r3 = E("vector", lambda e, a=SCC[i][64:128, :], x=pQ[i][0:64, :]:
                   e.tensor_tensor(a, x, SSQS[64:128, :], ALU.mult), inc="cV")
            pQ_free[i] = r3
            rp = E("gpsimd", lambda e, d=QHB[i][:, :], a=SCB[i][:, :], b2=SCC[i][:, :]:
                   e.tensor_add(d, a, b2), waits=(r3, qh_free[i]), inc="cP")
            scB_free[i] = rp
            scC_free[i] = rp
            qh_rope[i] = rp

        emit_qproj(0)
        dbg_attn_tok = None
        if dbg and s == 3:
            y_dma(dqh, QHB[0][:, :], waits=(qh_rope[0],))

        pend = [None]

        def emit_tail(ph, av_tok, rc_tok):
            nonlocal recB_free, pAV_free, pSUM_free, attn_done, strec_free
            bc = mm(pSUM[:, :], ONESROW[:, :], ST_REC[0:1, :], True, True,
                    waits=(rc_tok, init_tok), inc="cT")
            strec_free = bc
            evb = E("vector", lambda e: e.tensor_copy(RECB[:, :], pSUM[:, :]),
                    waits=(bc, recB_free), inc="cV")
            pSUM_free = evb
            nr = E("vector", lambda e, d=ATTN[:, 512 * ph:512 * (ph + 1)]:
                   e.tensor_tensor(d, pAV[:, :], RECB[:, :], ALU.mult),
                   waits=(evb, av_tok, attn_free), inc="cV")
            recB_free = nr
            pAV_free = nr
            attn_done = nr
            if dbg and s == 3 and ph == 0:
                y_dma(drecb, RECB[:, :], waits=(evb,))

        for g in range(HKV):
            gi = g % 2
            if g == 0:
                kb_tok[gi] = y_dma(kbuf[gi][:, 0:128 * c], kd[g, :, 0:128 * c],
                                   waits=(spill_done, kb_free[gi], init_tok))
                vb_tok[gi] = y_dma(vbuf[gi][:, 0:128 * c].rearrange("p (c2 d) -> p c2 d", c2=c),
                                   vd[g, :, 0:c, :],
                                   waits=(vb_free[gi],))
            for j in range(4):
                h = 4 * g + j
                i = h % 2
                if h + 1 < H:
                    emit_qproj(h + 1)
                if pend[0] is not None:
                    emit_tail(*pend[0])
                    pend[0] = None
                vbv = vbuf[gi][:, 0:128 * c].rearrange("p (c2 d) -> p c2 d", c2=c)
                sc_tok = None
                av_sum_tok = None
                pend_av = None

                def emit_avsum(pp, etok):
                    nonlocal av_sum_tok
                    ptv = pt_pair_view(pp % 2)
                    t2 = None
                    for q in (0, 1):
                        kt = 2 * pp + q
                        mm(pAV[:, :], vbv[:, kt, :], ptv[:, 512 * q:512 * (q + 1)],
                           kt == 0, kt == c - 1,
                           waits=(etok, vb_tok[gi], (pAV_free if kt == 0 else None)))
                        t2 = mm(pSUM[0:1, :], ONESB[:, :], ptv[:, 512 * q:512 * (q + 1)],
                                kt == 0, kt == c - 1,
                                waits=((pSUM_free if kt == 0 else None),),
                                inc=("cT" if q == 1 else None))
                    pt_free[pp % 2] = t2
                    av_sum_tok = t2

                for p in range(npair):
                    pi = p % 2
                    for q in (0, 1):
                        kt = 2 * p + q
                        sc_tok = mm(pSC[pi][:, 512 * q:512 * (q + 1)],
                                    kbuf[gi][:, 128 * kt:128 * (kt + 1)], QHB[i][:, :],
                                    True, True,
                                    waits=(kb_tok[gi], qh_rope[i], pSC_free[pi]),
                                    inc=("cT" if q == 1 else None))
                    ex = E("scalar", lambda e, d=pt_pair_view(pi), s2=pSC[pi][:, :]:
                           e.activation(d, s2, AF.Exp, scale=SCALE),
                           waits=(sc_tok, pt_free[pi]), inc="cS")
                    pSC_free[pi] = ex
                    if p >= npair - 4:
                        jj = 2 * (p - (npair - 4))
                        mview = MASKS[:, 512 * jj:512 * (jj + 2)]
                        ex = E("gpsimd", lambda e, d=pt_pair_view(pi), m2=mview:
                               e.tensor_mul(d, d, m2), waits=(ex, masks_tok), inc="cP")
                    emit_avsum(p, ex)
                kb_free[gi] = sc_tok if j == 3 else kb_free[gi]
                vb_free[gi] = av_sum_tok if j == 3 else vb_free[gi]
                qh_free[i] = sc_tok
                if j == 3 and g + 1 < HKV:
                    ni = (g + 1) % 2
                    kb_tok[ni] = y_dma(kbuf[ni][:, 0:128 * c], kd[g + 1, :, 0:128 * c],
                                       waits=(kb_free[ni],))
                    vb_tok[ni] = y_dma(vbuf[ni][:, 0:128 * c].rearrange("p (c2 d) -> p c2 d", c2=c),
                                       vd[g + 1, :, 0:c, :],
                                       waits=(vb_free[ni],))
                rc = E("vector", lambda e: e.reciprocal(ST_REC[:, :], pSUM[0:1, :]),
                       waits=(av_sum_tok, strec_free), inc="cV")
                if dbg and s == 3 and h == 0:
                    y_dma(drec, ST_REC[:, :], waits=(rc,))
                pend[0] = (h, av_sum_tok, rc)
        emit_tail(*pend[0])
        pend[0] = None
        xw_free = wb_free[1]
        if dbg and s == 3:
            dbg_attn_tok = y_dma(dattn, ATTN[:, :], waits=(attn_done,))

        attn_v = ATTN[:].rearrange("p (c2 t) -> p c2 t", c2=KCH)
        mu_ap = pSCa[0:1, 0:512]
        s2_ap = pSCa[0:1, 512:1024]
        last_exp_tok = pSC_free[0]
        last_exp_tok2 = pSC_free[1]
        res_tok = [None] * KCH

        def emit_stats(pm, pra, psq, stop):
            mm(mu_ap, ONESF[:, :], res_m(pm), pm == 0, stop,
               waits=(pra, init_tok, (last_exp_tok if pm == 0 else None),
                      (last_exp_tok2 if pm == 0 else None)))
            t3 = mm(s2_ap, ONESB[:, :], SQBF[pm % 2][:, :], pm == 0, stop,
                    waits=(psq,), inc="cT")
            sqbf_free[pm % 2] = t3
            return t3

        prev_stat = None
        for m in range(KCH):
            i = m % 2
            wtok = g_dma(WB[i][:].rearrange("p (c2 m2) -> p c2 m2", c2=KCH), wo[m],
                         waits=(wb_free[i],))
            xqtok = y_dma(XQR[i][:, :], xqt[128 * m:128 * (m + 1), 512 * s:512 * (s + 1)],
                          waits=(xq_free[i],))
            wv_ = WB[i][:].rearrange("p (c2 m2) -> p c2 m2", c2=KCH)
            t = None
            for c2 in range(KCH):
                t = mm(pQ[i][:, :], wv_[:, c2, :], attn_v[:, c2, :], c2 == 0, c2 == KCH - 1,
                       waits=(wtok, attn_done, pQ_free[i]),
                       inc=("cT" if c2 == KCH - 1 else None))
            wb_free[i] = t
            ra = E("vector", lambda e, d=res_m(m), s2=pQ[i][:, :], x=XQR[i][:, :]:
                   e.tensor_tensor(d, s2, x, ALU.add),
                   waits=(t, xqtok, attn_done), inc="cV")
            pQ_free[i] = ra
            xq_free[i] = ra
            res_tok[m] = ra
            sq = E("scalar", lambda e, d=SQBF[i][:, :], s2=res_m(m):
                   e.activation(d, s2, AF.Square), waits=(ra, sqbf_free[i]), inc="cS")
            if prev_stat is not None:
                emit_stats(prev_stat[0], prev_stat[1], prev_stat[2], False)
            prev_stat = (m, ra, sq)
        stats_tok = emit_stats(prev_stat[0], prev_stat[1], prev_stat[2], True)

        def ln_finalize(stats_tok, bcast_psum_free1, bcast_psum_free2):
            a1 = E("vector", lambda e: e.tensor_scalar(ST_MU[:, :], mu_ap, INV_D, None, ALU.mult),
                   waits=(stats_tok,), inc="cV")
            a2 = E("vector", lambda e: e.tensor_scalar(ST_VAR[:, :], s2_ap, INV_D, None, ALU.mult),
                   waits=(stats_tok,), inc="cV")
            a3 = E("vector", lambda e: e.tensor_tensor(ST_T[:, :], ST_MU[:, :], ST_MU[:, :], ALU.mult),
                   waits=(a1,), inc="cV")
            a4 = E("vector", lambda e: e.tensor_tensor(ST_VAR2[:, :], ST_VAR[:, :], ST_T[:, :], ALU.subtract),
                   waits=(a2, a3), inc="cV")
            a5 = E("scalar", lambda e: e.activation(ST_T[:, :], ST_VAR2[:, :], AF.Ln, bias=EPSV[0:1, 0:1]),
                   waits=(a4, init_tok), inc="cS")
            a6 = E("scalar", lambda e: e.activation(ST_RR[:, :], ST_T[:, :], AF.Exp, scale=-0.5),
                   waits=(a5,), inc="cS")
            a7 = E("vector", lambda e: e.tensor_tensor(ST_MUR[:, :], ST_MU[:, :], ST_RR[:, :], ALU.mult),
                   waits=(a6,), inc="cV")
            if dbg_ln_dump[0]:
                y_dma(dbg_ln_dump[1], ST_VAR[:, :], waits=(a4,))
                y_dma(dbg_ln_dump[2], ST_VAR2[:, :], waits=(a4,))
                y_dma(dbg_ln_dump[3], ST_T[:, :], waits=(a5,))
                y_dma(dbg_ln_dump[4], ST_RR[:, :], waits=(a6,))
                dbg_ln_dump[0] = False
            b1 = mm(pSCb[:, 0:512], ONESROW[:, :], ST_RR[0:1, :], True, True,
                    waits=(a6, bcast_psum_free1), inc="cT")
            b2 = mm(pSCb[:, 512:1024], ONESROW[:, :], ST_MUR[0:1, :], True, True,
                    waits=(a7, bcast_psum_free2), inc="cT")
            e1 = E("vector", lambda e: e.tensor_copy(RRB[:, :], pSCb[:, 0:512]),
                   waits=(b1, rrb_free), inc="cV")
            e2 = E("vector", lambda e: e.tensor_copy(MURB[:, :], pSCb[:, 512:1024]),
                   waits=(b2,), inc="cV")
            return a2, e1, e2

        dbg_res0_tok = dbg_y0_tok = None
        dbg_ln_dump = [False, None, None, None, None]
        if dbg and s == 3:
            dbg_res0_tok = y_dma(dres0, res_m(0), waits=(res_tok[0],))
            dbg_res0_tok = y_dma(dresall, RES[:, :], waits=(res_tok[KCH - 1], dbg_res0_tok))
            dbg_ln_dump = [True, ds2m, dvar, dlnv, drr2]
        a2_ln1, e1, e2 = ln_finalize(stats_tok, pSC_free[1], pSC_free[1])
        if dbg and s == 3:
            y_dma(dmu, ST_MU[:, :], waits=(e2,))
            y_dma(drr, ST_RR[:, :], waits=(e2,))
            d1 = y_dma(drrb, RRB[:, :], waits=(e1,))
            d2 = y_dma(dmurb, MURB[:, :], waits=(e2,))
            rrb_free = d2
        y_tok = [None] * KCH
        ybf_tok = [None] * KCH
        for m in range(KCH):
            t1 = E("vector", lambda e, s2=res_m(m): e.tensor_tensor(LNT[:, :], s2, RRB[:, :], ALU.mult),
                   waits=(e1, res_tok[m], lnt_free), inc="cV")
            t2 = E("vector", lambda e, d=res_m(m): e.tensor_tensor(d, LNT[:, :], MURB[:, :], ALU.subtract),
                   waits=(e2, t1, dbg_res0_tok), inc="cV")
            lnt_free = t2
            y_tok[m] = t2
            yb = E("gpsimd", lambda e, d=ATTN[:, 512 * m:512 * (m + 1)], s2=res_m(m):
                   e.tensor_copy(d, s2), waits=(t2, wb_free[0], wb_free[1], dbg_attn_tok), inc="cP")
            ybf_tok[m] = yb
        rrb_free = y_tok[KCH - 1]

        ybf_v = ATTN[:].rearrange("p (c2 t) -> p c2 t", c2=KCH)
        hs_v = HSB[:].rearrange("p (r t) -> p r t", r=RCH)
        hs_tok = [None] * RCH
        for r in range(RCH):
            i = r % 2
            wtok = g_dma(WB[i][:].rearrange("p (c2 m2) -> p c2 m2", c2=KCH), vf[r],
                         waits=(wb_free[i],))
            wv_ = WB[i][:].rearrange("p (c2 m2) -> p c2 m2", c2=KCH)
            t = None
            for c2 in range(KCH):
                t = mm(pQ[i][:, :], wv_[:, c2, :], ybf_v[:, c2, :], c2 == 0, c2 == KCH - 1,
                       waits=(wtok, ybf_tok[c2], pQ_free[i]),
                       inc=("cT" if c2 == KCH - 1 else None))
            wb_free[i] = t
            hv = E("vector", lambda e, d=hs_v[:, r, :], s2=pQ[i][:, :], sv=SVS[:, r:r + 1]:
                   e.tensor_scalar(d, s2, sv, None, ALU.mult),
                   waits=(t, sv_tok, hs_tok[r]), inc="cV")
            pQ_free[i] = hv
            hs_tok[r] = hv
        attn_free = t
        if dbg and s == 3:
            dbg_y0_tok = y_dma(dy0, res_m(0), waits=(y_tok[0],))
            y_dma(dh, HSB[:, :], waits=(hs_tok[RCH - 1],))
            y_dma(dybf, ATTN[:, :], waits=(ybf_tok[KCH - 1],))
        res2_tok = [None] * KCH

        def emit_stats2(pm, pra, psq, stop):
            mm(mu_ap, ONESF[:, :], res_m(pm), pm == 0, stop,
               waits=(pra, (a2_ln1 if pm == 0 else None)))
            t3 = mm(s2_ap, ONESB[:, :], SQBF[pm % 2][:, :], pm == 0, stop,
                    waits=(psq,), inc="cT")
            sqbf_free[pm % 2] = t3
            return t3

        prev_stat = None
        for m in range(KCH):
            i = m % 2
            wtok = g_dma(WB[i][:, 0:1024].rearrange("p (c2 m2) -> p c2 m2", c2=RCH), ut[m],
                         waits=(wb_free[i],))
            wv_ = WB[i][:, 0:1024].rearrange("p (c2 m2) -> p c2 m2", c2=RCH)
            t = None
            for c2 in range(RCH):
                t = mm(pQ[i][:, :], wv_[:, c2, :], hs_v[:, c2, :], c2 == 0, c2 == RCH - 1,
                       waits=(wtok, hs_tok[c2], pQ_free[i]),
                       inc=("cT" if c2 == RCH - 1 else None))
            wb_free[i] = t
            ra = E("vector", lambda e, d=res_m(m), s2=pQ[i][:, :]:
                   e.tensor_tensor(d, s2, d, ALU.add),
                   waits=(t, y_tok[m], ybf_tok[m],
                          (dbg_y0_tok if m == 0 else None)), inc="cV")
            pQ_free[i] = ra
            res2_tok[m] = ra
            sq = E("scalar", lambda e, d=SQBF[i][:, :], s2=res_m(m):
                   e.activation(d, s2, AF.Square), waits=(ra, sqbf_free[i]), inc="cS")
            if prev_stat is not None:
                emit_stats2(prev_stat[0], prev_stat[1], prev_stat[2], False)
            prev_stat = (m, ra, sq)
        stats_tok = emit_stats2(prev_stat[0], prev_stat[1], prev_stat[2], True)

        dbg_res2_tok = None
        if dbg and s == 3:
            dbg_res2_tok = y_dma(dres2, res_m(0), waits=(res2_tok[0],))
        a2_ln2, e1, e2 = ln_finalize(stats_tok, e1, e2)
        for m in range(KCH):
            t1 = E("vector", lambda e, s2=res_m(m): e.tensor_tensor(LNT[:, :], s2, RRB[:, :], ALU.mult),
                   waits=(e1, res2_tok[m], lnt_free), inc="cV")
            t2 = E("vector", lambda e, d=res_m(m): e.tensor_tensor(d, LNT[:, :], MURB[:, :], ALU.subtract),
                   waits=(e2, t1, (dbg_res2_tok if m == 0 else None)), inc="cV")
            lnt_free = t2
            ot = y_dma(zt[128 * m:128 * (m + 1), 512 * s:512 * (s + 1)], res_m(m),
                       waits=(t2,), sem="dO")
            out_toks[(s, m)] = ot
        rrb_free = t2

        if s + 1 < NSLOT:
            kb_free[0] = out_toks[(s, 3)]
            kb_free[1] = out_toks[(s, 3)]
            vb_free[0] = out_toks[(s, 15)]
            vb_free[1] = out_toks[(s, 15)]
            pt_free[0] = out_toks[(s, 17)]
            pt_free[1] = out_toks[(s, 17)]
            pSC_free[0] = a2_ln2
            pSC_free[1] = e2
            masks_free = stats_tok
            qrope_free = ("cV", bld.cnt["cV"])

    total_dO = bld.cnt["dO"]
    E("sync", lambda e: e.wait_ge(sems["dO"], total_dO))

    nops = {e: len(bld.ops[e]) for e in ENGINES}

    with nc.Block() as block:
        def make_runner(name):
            def run(e):
                for (ws, fn, inc, n) in bld.ops[name]:
                    for (s2, v) in ws:
                        e.wait_ge(sems[s2], v)
                    ins = fn(e)
                    if inc is not None:
                        ins.then_inc(sems[inc], n)
            return run
        block.tensor(make_runner("tensor"))
        block.scalar(make_runner("scalar"))
        block.vector(make_runner("vector"))
        block.gpsimd(make_runner("gpsimd"))
        block.sync(make_runner("sync"))

    stack.close()
    return nc, nops


def _rope_tables():
    inv = (1.0 / (10000.0 ** (np.arange(0, 64, dtype=np.float32) / np.float32(64.0)))).astype(np.float32)
    pos = np.arange(T, dtype=np.float32)
    ang = inv[:, None] * pos[None, :]
    cos = np.cos(ang).astype(np.float32)
    sin = np.sin(ang).astype(np.float32)
    cc = np.concatenate([cos, cos], axis=0)
    ss = np.concatenate([-sin, sin], axis=0)
    return cc, ss


def _masks_for_role(r):
    out = np.zeros((NSLOT, 8, 128, 512), dtype=np.float32)
    for s in range(NSLOT):
        J = CHUNKS[r][s]
        cpad = PADC[s]
        for jj in range(8):
            kt = cpad - 8 + jj
            if kt < 4 * J:
                out[s, jj] = 1.0
            elif kt < 4 * J + 4:
                q = 512 * J + np.arange(512)[None, :]
                k = 128 * kt + np.arange(128)[:, None]
                out[s, jj] = (q >= k).astype(np.float32)
    return out.astype(ml_dtypes.bfloat16)


def _chunked(w, nch):
    din, n = w.shape
    csz = n // nch
    return np.ascontiguousarray(
        w.reshape(din // 128, 128, nch, csz).transpose(2, 1, 0, 3))


def prep_in_maps(L0_attn_X, L0_attn_Wq, L0_attn_Wk, L0_attn_Wv, L0_attn_Wo,
                 L0_ffn_U, L0_ffn_S, L0_ffn_V):
    X = np.asarray(L0_attn_X, dtype=np.float32)
    wq = _chunked(np.asarray(L0_attn_Wq, dtype=np.float32), H)
    wk = _chunked(np.asarray(L0_attn_Wk, dtype=np.float32), HKV)
    wv = _chunked(np.asarray(L0_attn_Wv, dtype=np.float32), 2)
    wo = _chunked(np.asarray(L0_attn_Wo, dtype=np.float32), KCH)
    u = np.asarray(L0_ffn_U, dtype=np.float32)
    sv = np.asarray(L0_ffn_S, dtype=np.float32)
    vf = _chunked(np.asarray(L0_ffn_V, dtype=np.float32), RCH)
    ut = _chunked(np.ascontiguousarray(u.T), KCH)
    svt = np.ascontiguousarray(sv.reshape(RCH, 128).T)

    cc, ss = _rope_tables()
    masks = [_masks_for_role(0), _masks_for_role(1)]

    in_maps = []
    xts = [np.ascontiguousarray(X[b].T) for b in range(B)]
    xtls = [_chunked(xts[b], NTC) for b in range(B)]
    for cidx in range(8):
        b, r = cidx // 2, cidx % 2
        qcols = np.concatenate([np.arange(512 * J, 512 * (J + 1)) for J in CHUNKS[r]])
        xqt = np.ascontiguousarray(xts[b][:, qcols])
        in_maps.append({
            "xtl": xtls[b], "xql": _chunked(xqt, NSLOT), "xqt": xqt,
            "wq": wq, "wk": wk, "wv": wv, "wo": wo,
            "vf": vf, "ut": ut, "svt": svt,
            "ccin": cc, "ssin": ss,
            "ccq": np.ascontiguousarray(cc[:, qcols]),
            "ssq": np.ascontiguousarray(ss[:, qcols]),
            "maskin": masks[r],
        })
    return in_maps


def kernel(L0_attn_X, L0_attn_Wq, L0_attn_Wk, L0_attn_Wv, L0_attn_Wo,
           L0_ffn_U, L0_ffn_S, L0_ffn_V):
    in_maps = prep_in_maps(L0_attn_X, L0_attn_Wq, L0_attn_Wk, L0_attn_Wv,
                           L0_attn_Wo, L0_ffn_U, L0_ffn_S, L0_ffn_V)
    nc, _ = build_nc()
    res = run_bass_kernel_spmd(nc, in_maps, core_ids=list(range(8)))
    out = np.empty((B, T, D), dtype=np.float32)
    for cidx in range(8):
        ztr = res.results[cidx]["zt"]
        b, r = cidx // 2, cidx % 2
        for s2, J in enumerate(CHUNKS[r]):
            out[b, 512 * J:512 * (J + 1), :] = ztr[:, 512 * s2:512 * (s2 + 1)].T
    return out


if __name__ == "__main__":
    nc, nops = build_nc()
    print("instruction counts per engine:", nops)
    print("sbuf bytes remaining:", nc.sbuf_bytes_remaining)


# version 10
# speedup vs baseline: 1.4406x; 1.2501x over previous
import math
import numpy as np
import ml_dtypes
import concourse.bass as bass
import concourse.mybir as mybir
from concourse.bass_utils import run_bass_kernel_spmd

F32 = mybir.dt.float32
BF16 = mybir.dt.bfloat16
AF = mybir.ActivationFunctionType
ALU = mybir.AluOpType

B, T, D = 4, 4096, 4096
HKV, H, DH = 8, 32, 128
RANK = 1024
KCH = D // 128
RCH = RANK // 128
NTC = T // 512
NSLOT = 4
CHUNKS = [[0, 3, 4, 7], [1, 2, 5, 6]]
PADC = [8, 16, 24, 32]
SCALE = 1.0 / math.sqrt(128.0)
INV_D = 1.0 / 4096.0
EPS = 1e-5

ENGINES = ("tensor", "scalar", "vector", "gpsimd", "sync")
SEMS = ("dW", "dY", "dK", "dO", "cT", "cS", "cV", "cP")


class Bld:
    def __init__(self):
        self.ops = {e: [] for e in ENGINES}
        self.cnt = {s: 0 for s in SEMS}
        self.lw = {e: {s: 0 for s in SEMS} for e in ENGINES}

    def emit(self, eng, fn, waits=(), inc=None, n=1):
        ws = []
        for t in waits:
            if t is None:
                continue
            s, v = t
            if self.lw[eng][s] < v:
                self.lw[eng][s] = v
                ws.append((s, v))
        tok = None
        if inc is not None:
            self.cnt[inc] += n
            tok = (inc, self.cnt[inc])
        self.ops[eng].append((ws, fn, inc, n))
        return tok


def build_nc(dbg=False):
    nc = bass.Bass("TRN2", target_bir_lowering=False, debug=False)
    KIND_KV = "ExternalOutput" if dbg else "Internal"

    xtl = nc.dram_tensor("xtl", [NTC, 128, KCH, 512], F32, kind="ExternalInput").ap()
    xql = nc.dram_tensor("xql", [NSLOT, 128, KCH, 512], F32, kind="ExternalInput").ap()
    xqt = nc.dram_tensor("xqt", [D, NSLOT * 512], F32, kind="ExternalInput").ap()
    wq = nc.dram_tensor("wq", [H, 128, KCH, 128], F32, kind="ExternalInput").ap()
    wk = nc.dram_tensor("wk", [HKV, 128, KCH, 128], F32, kind="ExternalInput").ap()
    wv = nc.dram_tensor("wv", [2, 128, KCH, 512], F32, kind="ExternalInput").ap()
    wo = nc.dram_tensor("wo", [KCH, 128, KCH, 128], F32, kind="ExternalInput").ap()
    vf = nc.dram_tensor("vf", [RCH, 128, KCH, 128], F32, kind="ExternalInput").ap()
    ut = nc.dram_tensor("ut", [KCH, 128, RCH, 128], F32, kind="ExternalInput").ap()
    svt = nc.dram_tensor("svt", [128, RCH], F32, kind="ExternalInput").ap()
    ccin = nc.dram_tensor("ccin", [128, T], F32, kind="ExternalInput").ap()
    ssin = nc.dram_tensor("ssin", [128, T], F32, kind="ExternalInput").ap()
    ccq = nc.dram_tensor("ccq", [128, NSLOT * 512], F32, kind="ExternalInput").ap()
    ssq = nc.dram_tensor("ssq", [128, NSLOT * 512], F32, kind="ExternalInput").ap()
    maskin = nc.dram_tensor("maskin", [NSLOT, 8, 128, 512], BF16, kind="ExternalInput").ap()
    kd = nc.dram_tensor("kd", [HKV, 128, T], BF16, kind=KIND_KV).ap()
    vd = nc.dram_tensor("vd", [HKV, 128, 32, 128], BF16, kind=KIND_KV).ap()
    if dbg:
        dqh = nc.dram_tensor("dqh", [128, 512], BF16, kind="ExternalOutput").ap()
        dattn = nc.dram_tensor("dattn", [128, 16384], BF16, kind="ExternalOutput").ap()
        drec = nc.dram_tensor("drec", [1, 512], F32, kind="ExternalOutput").ap()
        drecb = nc.dram_tensor("drecb", [128, 512], F32, kind="ExternalOutput").ap()
        dres0 = nc.dram_tensor("dres0", [128, 512], F32, kind="ExternalOutput").ap()
        dmu = nc.dram_tensor("dmu", [1, 512], F32, kind="ExternalOutput").ap()
        drr = nc.dram_tensor("drr", [1, 512], F32, kind="ExternalOutput").ap()
        dy0 = nc.dram_tensor("dy0", [128, 512], F32, kind="ExternalOutput").ap()
        dh = nc.dram_tensor("dh", [128, 4096], BF16, kind="ExternalOutput").ap()
        dres2 = nc.dram_tensor("dres2", [128, 512], F32, kind="ExternalOutput").ap()
        drrb = nc.dram_tensor("drrb", [128, 512], F32, kind="ExternalOutput").ap()
        dmurb = nc.dram_tensor("dmurb", [128, 512], F32, kind="ExternalOutput").ap()
        dybf = nc.dram_tensor("dybf", [128, 16384], BF16, kind="ExternalOutput").ap()
        dresall = nc.dram_tensor("dresall", [128, 16384], F32, kind="ExternalOutput").ap()
        ds2m = nc.dram_tensor("ds2m", [1, 512], F32, kind="ExternalOutput").ap()
        dvar = nc.dram_tensor("dvar", [1, 512], F32, kind="ExternalOutput").ap()
        dlnv = nc.dram_tensor("dlnv", [1, 512], F32, kind="ExternalOutput").ap()
        drr2 = nc.dram_tensor("drr2", [1, 512], F32, kind="ExternalOutput").ap()
    zt = nc.dram_tensor("zt", [D, NSLOT * 512], F32, kind="ExternalOutput").ap()

    bld = Bld()
    E = bld.emit

    import contextlib
    stack = contextlib.ExitStack()
    S = lambda shape, dt, name: stack.enter_context(nc.sbuf_tensor(name, shape, dt))

    RES = S([128, 16384], F32, "RES")
    XW = S([128, 16384], BF16, "XW")
    ATTN = S([128, 16384], BF16, "ATTN")
    WB = [S([128, 4096], BF16, "WB0"), S([128, 4096], BF16, "WB1")]
    MH = S([128, 4096], BF16, "MH")
    MASKS = MH
    HSB = MH
    KOUT = [S([128, 512], BF16, "KOUT0"), S([128, 512], BF16, "KOUT1")]
    QHB = KOUT
    CCQS = S([128, 512], F32, "CCQS")
    SSQS = S([128, 512], F32, "SSQS")
    XQR = [S([128, 512], F32, "XQR0"), S([128, 512], F32, "XQR1")]
    VOUT = [XQR[0][:, 0:256].bitcast(BF16), XQR[1][:, 0:256].bitcast(BF16)]
    SCA = [S([128, 512], F32, "SCA0"), S([128, 512], F32, "SCA1")]
    SCB = [S([128, 512], F32, "SCB0"), S([128, 512], F32, "SCB1")]
    SCC = [S([128, 512], F32, "SCC0"), S([128, 512], F32, "SCC1")]
    RECB = S([128, 512], F32, "RECB")
    RRB = S([128, 512], F32, "RRB")
    MURB = S([128, 512], F32, "MURB")
    LNT = SCA[0]
    SQBF = [S([128, 512], BF16, "SQBF0"), S([128, 512], BF16, "SQBF1")]
    ST_REC = S([1, 512], F32, "ST_REC")
    ST_MU = S([1, 512], F32, "ST_MU")
    ST_VAR = S([1, 512], F32, "ST_VAR")
    ST_VAR2 = S([1, 512], F32, "ST_VAR2")
    ST_RR = S([1, 512], F32, "ST_RR")
    ST_MUR = S([1, 512], F32, "ST_MUR")
    ST_T = S([1, 512], F32, "ST_T")
    ONESB = S([128, 1], BF16, "ONESB")
    ONESF = S([128, 1], F32, "ONESF")
    ONESROW = S([1, 128], F32, "ONESROW")
    EPSV = S([128, 1], F32, "EPSV")
    SVS = S([128, RCH], F32, "SVS")

    cc_t = RES[:, 0:4096]
    ss_t = RES[:, 4096:8192]
    xbfA = RES[:, 8192:16384].bitcast(BF16)
    kbuf = [RES[:, 0:2048].bitcast(BF16), RES[:, 2048:4096].bitcast(BF16)]
    vbuf = [RES[:, 4096:6144].bitcast(BF16), RES[:, 6144:8192].bitcast(BF16)]
    ptp = [RES[:, 8192:8704].bitcast(BF16), RES[:, 8704:9216].bitcast(BF16)]

    def res_m(m):
        return RES[:, 512 * m:512 * (m + 1)]

    P = lambda shape, name: stack.enter_context(nc.psum_tensor(name, shape, F32))
    pQ = [P([128, 512], "pQ0"), P([128, 512], "pQ1")]
    pSCa = P([128, 1024], "pSCa")
    pSCb = P([128, 1024], "pSCb")
    pAV = P([128, 512], "pAV")
    pSUM = P([128, 512], "pSUM")

    sems = {s: stack.enter_context(nc.semaphore(s)) for s in SEMS}

    def g_dma(dst, src, waits=()):
        return E("gpsimd", lambda e, d=dst, s2=src: e.dma_start(out=d, in_=s2),
                 waits=waits, inc="dW", n=16)

    def y_dma(dst, src, waits=(), sem="dY"):
        return E("sync", lambda e, d=dst, s2=src: e.dma_start(out=d, in_=s2),
                 waits=waits, inc=sem, n=16)

    def mm(dst, lh, rh, start, stop, waits=(), inc=None):
        return E("tensor",
                 lambda e, d=dst, l=lh, r=rh, st=start, sp=stop:
                 e.matmul(d, l, r, start=st, stop=sp, skip_group_check=True),
                 waits=waits, inc=inc)

    E("gpsimd", lambda e: e.memset(ONESB[:, :], 1.0))
    E("gpsimd", lambda e: e.memset(ONESF[:, :], 1.0))
    E("gpsimd", lambda e: e.memset(ONESROW[:, :], 1.0))
    init_tok = E("gpsimd", lambda e: e.memset(EPSV[:, :], EPS), inc="cP")
    cc_tok = y_dma(cc_t, ccin)
    ss_tok = y_dma(ss_t, ssin)
    sv_tok = y_dma(SVS[:, :], svt)

    wb_free = [None, None]
    pQ_free = [None, None]
    kout_free = [None, None]
    vout_free = [None, None]
    scA_free = [None, None]
    scB_free = [None, None]
    scC_free = [None, None]
    qh_rope = [None, None]
    qh_free = [None, None]
    xbfA_free = None
    xw_free = None

    for Ti in range(NTC):
        xbfA_tok = g_dma(xbfA.rearrange("p (c t) -> p c t", c=KCH), xtl[Ti],
                         waits=(xbfA_free, cc_tok, ss_tok))
        for g in range(HKV):
            i = g % 2
            wtok = g_dma(WB[i][:].rearrange("p (c m) -> p c m", c=KCH), wk[g],
                         waits=(wb_free[i],))
            wv_ = WB[i][:].rearrange("p (c m) -> p c m", c=KCH)
            xv_ = xbfA.rearrange("p (c t) -> p c t", c=KCH)
            t = None
            for c in range(KCH):
                t = mm(pQ[i][:, :], wv_[:, c, :], xv_[:, c, :], c == 0, c == KCH - 1,
                       waits=(wtok, xbfA_tok, pQ_free[i]),
                       inc=("cT" if c == KCH - 1 else None))
            wb_free[i] = t
            kmm = t
            ccs = cc_t[:, 512 * Ti:512 * (Ti + 1)]
            sss = ss_t[:, 512 * Ti:512 * (Ti + 1)]
            E("vector", lambda e, a=SCB[i][:, :], x=pQ[i][:, :], c2=ccs:
              e.tensor_tensor(a, x, c2, ALU.mult), waits=(kmm, scB_free[i], cc_tok, ss_tok))
            E("vector", lambda e, a=SCC[i][0:64, :], x=pQ[i][64:128, :], s2=sss[0:64, :]:
              e.tensor_tensor(a, x, s2, ALU.mult), waits=(scC_free[i],))
            r3 = E("vector", lambda e, a=SCC[i][64:128, :], x=pQ[i][0:64, :], s2=sss[64:128, :]:
                   e.tensor_tensor(a, x, s2, ALU.mult), inc="cV")
            pQ_free[i] = r3
            rp = E("gpsimd", lambda e, d=KOUT[i][:, :], a=SCB[i][:, :], b2=SCC[i][:, :]:
                   e.tensor_add(d, a, b2), waits=(r3, kout_free[i]), inc="cP")
            scB_free[i] = rp
            scC_free[i] = rp
            sp = y_dma(kd[g, :, 512 * Ti:512 * (Ti + 1)], KOUT[i][:, :], waits=(rp,), sem="dK")
            kout_free[i] = sp
        for hv in range(2):
            wvh_tok = g_dma(XW[:].rearrange("p (c n) -> p c n", c=KCH), wv[hv],
                            waits=(xw_free,))
            xwv = XW[:].rearrange("p (c n) -> p c n", c=KCH)
            xv_ = xbfA.rearrange("p (c t) -> p c t", c=KCH)
            for tt in range(4):
                i = tt % 2
                t = None
                for c in range(KCH):
                    t = mm(pQ[i][:, :], xv_[:, c, 128 * tt:128 * (tt + 1)], xwv[:, c, :],
                           c == 0, c == KCH - 1,
                           waits=(wvh_tok, xbfA_tok, pQ_free[i]),
                           inc=("cT" if c == KCH - 1 else None))
                xw_free = t
                ev = E("vector", lambda e, d=VOUT[i][:, :], s2=pQ[i][:, :]: e.tensor_copy(d, s2),
                       waits=(t, vout_free[i]), inc="cV")
                pQ_free[i] = ev
                last = None
                for j in range(4):
                    gg = 4 * hv + j
                    last = y_dma(vd[gg, :, 4 * Ti + tt, :],
                                 VOUT[i][:, 128 * j:128 * (j + 1)], waits=(ev,), sem="dK")
                vout_free[i] = last
        xbfA_free = xw_free

    spill_done = ("dK", bld.cnt["dK"])

    qh_free = [kout_free[0], kout_free[1]]
    xq_free = [vout_free[0], vout_free[1]]

    kb_tok = [None, None]
    vb_tok = [None, None]
    kb_free = [None, None]
    vb_free = [None, None]
    pt_free = [None, None]
    pSC_free = [None, None]
    pAV_free = init_tok
    pSUM_free = init_tok
    recB_free = None
    strec_free = init_tok
    masks_tok = None
    masks_free = None
    qrope_tok = None
    qrope_free = None
    attn_free = None
    xq_free = [None, None]
    sqbf_free = [None, None]
    lnt_free = None
    rrb_free = None
    out_toks = {}
    attn_done = None
    wq_toks = [None, None]
    xbfB_tok = None

    pSC = [pSCa, pSCb]

    def pt_pair_view(i):
        return ptp[i]

    for s in range(NSLOT):
        c = PADC[s]
        npair = c // 2
        masks_tok = y_dma(MASKS[:].rearrange("p (j n) -> p j n", j=8),
                          maskin[s].rearrange("j p n -> p j n"),
                          waits=(masks_free,))
        qrope_tok1 = y_dma(CCQS[:, :], ccq[:, 512 * s:512 * (s + 1)], waits=(qrope_free,))
        qrope_tok = y_dma(SSQS[:, :], ssq[:, 512 * s:512 * (s + 1)], waits=(qrope_free,))
        xbfB_tok = g_dma(XW[:].rearrange("p (c2 t) -> p c2 t", c2=KCH), xql[s],
                         waits=(xw_free,))
        xbf_v = XW[:].rearrange("p (c2 t) -> p c2 t", c2=KCH)

        def emit_qproj(h):
            i = h % 2
            wtok = g_dma(WB[i][:].rearrange("p (c2 m) -> p c2 m", c2=KCH), wq[h],
                         waits=(wb_free[i],))
            wq_toks[i] = wtok
            wv_ = WB[i][:].rearrange("p (c2 m) -> p c2 m", c2=KCH)
            t = None
            for c2 in range(KCH):
                t = mm(pQ[i][:, :], wv_[:, c2, :], xbf_v[:, c2, :], c2 == 0, c2 == KCH - 1,
                       waits=(wtok, xbfB_tok, pQ_free[i]),
                       inc=("cT" if c2 == KCH - 1 else None))
            wb_free[i] = t
            E("vector", lambda e, a=SCB[i][:, :], x=pQ[i][:, :]:
              e.tensor_tensor(a, x, CCQS[:, :], ALU.mult),
              waits=(t, scB_free[i], qrope_tok, qrope_tok1))
            E("vector", lambda e, a=SCC[i][0:64, :], x=pQ[i][64:128, :]:
              e.tensor_tensor(a, x, SSQS[0:64, :], ALU.mult), waits=(scC_free[i],))
            r3 = E("vector", lambda e, a=SCC[i][64:128, :], x=pQ[i][0:64, :]:
                   e.tensor_tensor(a, x, SSQS[64:128, :], ALU.mult), inc="cV")
            pQ_free[i] = r3
            rp = E("gpsimd", lambda e, d=QHB[i][:, :], a=SCB[i][:, :], b2=SCC[i][:, :]:
                   e.tensor_add(d, a, b2), waits=(r3, qh_free[i]), inc="cP")
            scB_free[i] = rp
            scC_free[i] = rp
            qh_rope[i] = rp

        emit_qproj(0)
        dbg_attn_tok = None
        if dbg and s == 3:
            y_dma(dqh, QHB[0][:, :], waits=(qh_rope[0],))

        pend = [None]

        def emit_tail(ph, av_tok, rc_tok):
            nonlocal recB_free, pAV_free, pSUM_free, attn_done, strec_free
            bc = mm(pSUM[:, :], ONESROW[:, :], ST_REC[0:1, :], True, True,
                    waits=(rc_tok, init_tok), inc="cT")
            strec_free = bc
            evb = E("vector", lambda e: e.tensor_copy(RECB[:, :], pSUM[:, :]),
                    waits=(bc, recB_free), inc="cV")
            pSUM_free = evb
            nr = E("vector", lambda e, d=ATTN[:, 512 * ph:512 * (ph + 1)]:
                   e.tensor_tensor(d, pAV[:, :], RECB[:, :], ALU.mult),
                   waits=(evb, av_tok, attn_free), inc="cV")
            recB_free = nr
            pAV_free = nr
            attn_done = nr
            if dbg and s == 3 and ph == 0:
                y_dma(drecb, RECB[:, :], waits=(evb,))

        for g in range(HKV):
            gi = g % 2
            if g == 0:
                kb_tok[gi] = y_dma(kbuf[gi][:, 0:128 * c], kd[g, :, 0:128 * c],
                                   waits=(spill_done, kb_free[gi], init_tok))
                vb_tok[gi] = y_dma(vbuf[gi][:, 0:128 * c].rearrange("p (c2 d) -> p c2 d", c2=c),
                                   vd[g, :, 0:c, :],
                                   waits=(vb_free[gi],))
            for j in range(4):
                h = 4 * g + j
                i = h % 2
                if h + 1 < H:
                    emit_qproj(h + 1)
                if pend[0] is not None:
                    emit_tail(*pend[0])
                    pend[0] = None
                vbv = vbuf[gi][:, 0:128 * c].rearrange("p (c2 d) -> p c2 d", c2=c)
                sc_tok = None
                av_sum_tok = None
                pend_av = None

                def emit_avsum(pp, etok):
                    nonlocal av_sum_tok
                    ptv = pt_pair_view(pp % 2)
                    t2 = None
                    for q in (0, 1):
                        kt = 2 * pp + q
                        mm(pAV[:, :], vbv[:, kt, :], ptv[:, 512 * q:512 * (q + 1)],
                           kt == 0, kt == c - 1,
                           waits=(etok, vb_tok[gi], (pAV_free if kt == 0 else None)))
                        t2 = mm(pSUM[0:1, :], ONESB[:, :], ptv[:, 512 * q:512 * (q + 1)],
                                kt == 0, kt == c - 1,
                                waits=((pSUM_free if kt == 0 else None),),
                                inc=("cT" if q == 1 else None))
                    pt_free[pp % 2] = t2
                    av_sum_tok = t2

                for p in range(npair):
                    pi = p % 2
                    for q in (0, 1):
                        kt = 2 * p + q
                        sc_tok = mm(pSC[pi][:, 512 * q:512 * (q + 1)],
                                    kbuf[gi][:, 128 * kt:128 * (kt + 1)], QHB[i][:, :],
                                    True, True,
                                    waits=(kb_tok[gi], qh_rope[i], pSC_free[pi]),
                                    inc=("cT" if q == 1 else None))
                    ex = E("scalar", lambda e, d=pt_pair_view(pi), s2=pSC[pi][:, :]:
                           e.activation(d, s2, AF.Exp, scale=SCALE),
                           waits=(sc_tok, pt_free[pi]), inc="cS")
                    pSC_free[pi] = ex
                    if p >= npair - 4:
                        jj = 2 * (p - (npair - 4))
                        mview = MASKS[:, 512 * jj:512 * (jj + 2)]
                        ex = E("gpsimd", lambda e, d=pt_pair_view(pi), m2=mview:
                               e.tensor_mul(d, d, m2), waits=(ex, masks_tok), inc="cP")
                    if pend_av is not None:
                        emit_avsum(*pend_av)
                    pend_av = (p, ex)
                emit_avsum(*pend_av)
                kb_free[gi] = sc_tok if j == 3 else kb_free[gi]
                vb_free[gi] = av_sum_tok if j == 3 else vb_free[gi]
                qh_free[i] = sc_tok
                if j == 3 and g + 1 < HKV:
                    ni = (g + 1) % 2
                    kb_tok[ni] = y_dma(kbuf[ni][:, 0:128 * c], kd[g + 1, :, 0:128 * c],
                                       waits=(kb_free[ni],))
                    vb_tok[ni] = y_dma(vbuf[ni][:, 0:128 * c].rearrange("p (c2 d) -> p c2 d", c2=c),
                                       vd[g + 1, :, 0:c, :],
                                       waits=(vb_free[ni],))
                rc = E("vector", lambda e: e.reciprocal(ST_REC[:, :], pSUM[0:1, :]),
                       waits=(av_sum_tok, strec_free), inc="cV")
                if dbg and s == 3 and h == 0:
                    y_dma(drec, ST_REC[:, :], waits=(rc,))
                pend[0] = (h, av_sum_tok, rc)
        emit_tail(*pend[0])
        pend[0] = None
        xw_free = wb_free[1]
        if dbg and s == 3:
            dbg_attn_tok = y_dma(dattn, ATTN[:, :], waits=(attn_done,))

        attn_v = ATTN[:].rearrange("p (c2 t) -> p c2 t", c2=KCH)
        mu_ap = pSCa[0:1, 0:512]
        s2_ap = pSCa[0:1, 512:1024]
        last_exp_tok = pSC_free[0]
        last_exp_tok2 = pSC_free[1]
        res_tok = [None] * KCH

        def emit_stats(pm, pra, psq, stop):
            mm(mu_ap, ONESF[:, :], res_m(pm), pm == 0, stop,
               waits=(pra, init_tok, (last_exp_tok if pm == 0 else None),
                      (last_exp_tok2 if pm == 0 else None)))
            t3 = mm(s2_ap, ONESB[:, :], SQBF[pm % 2][:, :], pm == 0, stop,
                    waits=(psq,), inc="cT")
            sqbf_free[pm % 2] = t3
            return t3

        prev_stat = None
        for m in range(KCH):
            i = m % 2
            wtok = g_dma(WB[i][:].rearrange("p (c2 m2) -> p c2 m2", c2=KCH), wo[m],
                         waits=(wb_free[i],))
            xqtok = y_dma(XQR[i][:, :], xqt[128 * m:128 * (m + 1), 512 * s:512 * (s + 1)],
                          waits=(xq_free[i],))
            wv_ = WB[i][:].rearrange("p (c2 m2) -> p c2 m2", c2=KCH)
            t = None
            for c2 in range(KCH):
                t = mm(pQ[i][:, :], wv_[:, c2, :], attn_v[:, c2, :], c2 == 0, c2 == KCH - 1,
                       waits=(wtok, attn_done, pQ_free[i]),
                       inc=("cT" if c2 == KCH - 1 else None))
            wb_free[i] = t
            ra = E("vector", lambda e, d=res_m(m), s2=pQ[i][:, :], x=XQR[i][:, :]:
                   e.tensor_tensor(d, s2, x, ALU.add),
                   waits=(t, xqtok, attn_done), inc="cV")
            pQ_free[i] = ra
            xq_free[i] = ra
            res_tok[m] = ra
            sq = E("scalar", lambda e, d=SQBF[i][:, :], s2=res_m(m):
                   e.activation(d, s2, AF.Square), waits=(ra, sqbf_free[i]), inc="cS")
            if prev_stat is not None:
                emit_stats(prev_stat[0], prev_stat[1], prev_stat[2], False)
            prev_stat = (m, ra, sq)
        stats_tok = emit_stats(prev_stat[0], prev_stat[1], prev_stat[2], True)

        def ln_finalize(stats_tok, bcast_psum_free1, bcast_psum_free2):
            a1 = E("vector", lambda e: e.tensor_scalar(ST_MU[:, :], mu_ap, INV_D, None, ALU.mult),
                   waits=(stats_tok,), inc="cV")
            a2 = E("vector", lambda e: e.tensor_scalar(ST_VAR[:, :], s2_ap, INV_D, None, ALU.mult),
                   waits=(stats_tok,), inc="cV")
            a3 = E("vector", lambda e: e.tensor_tensor(ST_T[:, :], ST_MU[:, :], ST_MU[:, :], ALU.mult),
                   waits=(a1,), inc="cV")
            a4 = E("vector", lambda e: e.tensor_tensor(ST_VAR2[:, :], ST_VAR[:, :], ST_T[:, :], ALU.subtract),
                   waits=(a2, a3), inc="cV")
            a5 = E("scalar", lambda e: e.activation(ST_T[:, :], ST_VAR2[:, :], AF.Ln, bias=EPSV[0:1, 0:1]),
                   waits=(a4, init_tok), inc="cS")
            a6 = E("scalar", lambda e: e.activation(ST_RR[:, :], ST_T[:, :], AF.Exp, scale=-0.5),
                   waits=(a5,), inc="cS")
            a7 = E("vector", lambda e: e.tensor_tensor(ST_MUR[:, :], ST_MU[:, :], ST_RR[:, :], ALU.mult),
                   waits=(a6,), inc="cV")
            if dbg_ln_dump[0]:
                y_dma(dbg_ln_dump[1], ST_VAR[:, :], waits=(a4,))
                y_dma(dbg_ln_dump[2], ST_VAR2[:, :], waits=(a4,))
                y_dma(dbg_ln_dump[3], ST_T[:, :], waits=(a5,))
                y_dma(dbg_ln_dump[4], ST_RR[:, :], waits=(a6,))
                dbg_ln_dump[0] = False
            b1 = mm(pSCb[:, 0:512], ONESROW[:, :], ST_RR[0:1, :], True, True,
                    waits=(a6, bcast_psum_free1), inc="cT")
            b2 = mm(pSCb[:, 512:1024], ONESROW[:, :], ST_MUR[0:1, :], True, True,
                    waits=(a7, bcast_psum_free2), inc="cT")
            e1 = E("vector", lambda e: e.tensor_copy(RRB[:, :], pSCb[:, 0:512]),
                   waits=(b1, rrb_free), inc="cV")
            e2 = E("vector", lambda e: e.tensor_copy(MURB[:, :], pSCb[:, 512:1024]),
                   waits=(b2,), inc="cV")
            return a2, e1, e2

        dbg_res0_tok = dbg_y0_tok = None
        dbg_ln_dump = [False, None, None, None, None]
        if dbg and s == 3:
            dbg_res0_tok = y_dma(dres0, res_m(0), waits=(res_tok[0],))
            dbg_res0_tok = y_dma(dresall, RES[:, :], waits=(res_tok[KCH - 1], dbg_res0_tok))
            dbg_ln_dump = [True, ds2m, dvar, dlnv, drr2]
        a2_ln1, e1, e2 = ln_finalize(stats_tok, pSC_free[1], pSC_free[1])
        if dbg and s == 3:
            y_dma(dmu, ST_MU[:, :], waits=(e2,))
            y_dma(drr, ST_RR[:, :], waits=(e2,))
            d1 = y_dma(drrb, RRB[:, :], waits=(e1,))
            d2 = y_dma(dmurb, MURB[:, :], waits=(e2,))
            rrb_free = d2
        y_tok = [None] * KCH
        ybf_tok = [None] * KCH
        for m in range(KCH):
            t1 = E("vector", lambda e, s2=res_m(m): e.tensor_tensor(LNT[:, :], s2, RRB[:, :], ALU.mult),
                   waits=(e1, res_tok[m], lnt_free), inc="cV")
            t2 = E("vector", lambda e, d=res_m(m): e.tensor_tensor(d, LNT[:, :], MURB[:, :], ALU.subtract),
                   waits=(e2, t1, dbg_res0_tok), inc="cV")
            lnt_free = t2
            y_tok[m] = t2
            yb = E("gpsimd", lambda e, d=ATTN[:, 512 * m:512 * (m + 1)], s2=res_m(m):
                   e.tensor_copy(d, s2), waits=(t2, wb_free[0], wb_free[1], dbg_attn_tok), inc="cP")
            ybf_tok[m] = yb
        rrb_free = y_tok[KCH - 1]

        ybf_v = ATTN[:].rearrange("p (c2 t) -> p c2 t", c2=KCH)
        hs_v = HSB[:].rearrange("p (r t) -> p r t", r=RCH)
        hs_tok = [None] * RCH
        for r in range(RCH):
            i = r % 2
            wtok = g_dma(WB[i][:].rearrange("p (c2 m2) -> p c2 m2", c2=KCH), vf[r],
                         waits=(wb_free[i],))
            wv_ = WB[i][:].rearrange("p (c2 m2) -> p c2 m2", c2=KCH)
            t = None
            for c2 in range(KCH):
                t = mm(pQ[i][:, :], wv_[:, c2, :], ybf_v[:, c2, :], c2 == 0, c2 == KCH - 1,
                       waits=(wtok, ybf_tok[c2], pQ_free[i]),
                       inc=("cT" if c2 == KCH - 1 else None))
            wb_free[i] = t
            hv = E("vector", lambda e, d=hs_v[:, r, :], s2=pQ[i][:, :], sv=SVS[:, r:r + 1]:
                   e.tensor_scalar(d, s2, sv, None, ALU.mult),
                   waits=(t, sv_tok, hs_tok[r]), inc="cV")
            pQ_free[i] = hv
            hs_tok[r] = hv
        attn_free = t
        if dbg and s == 3:
            dbg_y0_tok = y_dma(dy0, res_m(0), waits=(y_tok[0],))
            y_dma(dh, HSB[:, :], waits=(hs_tok[RCH - 1],))
            y_dma(dybf, ATTN[:, :], waits=(ybf_tok[KCH - 1],))
        res2_tok = [None] * KCH

        def emit_stats2(pm, pra, psq, stop):
            mm(mu_ap, ONESF[:, :], res_m(pm), pm == 0, stop,
               waits=(pra, (a2_ln1 if pm == 0 else None)))
            t3 = mm(s2_ap, ONESB[:, :], SQBF[pm % 2][:, :], pm == 0, stop,
                    waits=(psq,), inc="cT")
            sqbf_free[pm % 2] = t3
            return t3

        prev_stat = None
        for m in range(KCH):
            i = m % 2
            wtok = g_dma(WB[i][:, 0:1024].rearrange("p (c2 m2) -> p c2 m2", c2=RCH), ut[m],
                         waits=(wb_free[i],))
            wv_ = WB[i][:, 0:1024].rearrange("p (c2 m2) -> p c2 m2", c2=RCH)
            t = None
            for c2 in range(RCH):
                t = mm(pQ[i][:, :], wv_[:, c2, :], hs_v[:, c2, :], c2 == 0, c2 == RCH - 1,
                       waits=(wtok, hs_tok[c2], pQ_free[i]),
                       inc=("cT" if c2 == RCH - 1 else None))
            wb_free[i] = t
            ra = E("vector", lambda e, d=res_m(m), s2=pQ[i][:, :]:
                   e.tensor_tensor(d, s2, d, ALU.add),
                   waits=(t, y_tok[m], ybf_tok[m],
                          (dbg_y0_tok if m == 0 else None)), inc="cV")
            pQ_free[i] = ra
            res2_tok[m] = ra
            sq = E("scalar", lambda e, d=SQBF[i][:, :], s2=res_m(m):
                   e.activation(d, s2, AF.Square), waits=(ra, sqbf_free[i]), inc="cS")
            if prev_stat is not None:
                emit_stats2(prev_stat[0], prev_stat[1], prev_stat[2], False)
            prev_stat = (m, ra, sq)
        stats_tok = emit_stats2(prev_stat[0], prev_stat[1], prev_stat[2], True)

        dbg_res2_tok = None
        if dbg and s == 3:
            dbg_res2_tok = y_dma(dres2, res_m(0), waits=(res2_tok[0],))
        a2_ln2, e1, e2 = ln_finalize(stats_tok, e1, e2)
        for m in range(KCH):
            t1 = E("vector", lambda e, s2=res_m(m): e.tensor_tensor(LNT[:, :], s2, RRB[:, :], ALU.mult),
                   waits=(e1, res2_tok[m], lnt_free), inc="cV")
            t2 = E("vector", lambda e, d=res_m(m): e.tensor_tensor(d, LNT[:, :], MURB[:, :], ALU.subtract),
                   waits=(e2, t1, (dbg_res2_tok if m == 0 else None)), inc="cV")
            lnt_free = t2
            ot = y_dma(zt[128 * m:128 * (m + 1), 512 * s:512 * (s + 1)], res_m(m),
                       waits=(t2,), sem="dO")
            out_toks[(s, m)] = ot
        rrb_free = t2

        if s + 1 < NSLOT:
            kb_free[0] = out_toks[(s, 3)]
            kb_free[1] = out_toks[(s, 3)]
            vb_free[0] = out_toks[(s, 15)]
            vb_free[1] = out_toks[(s, 15)]
            pt_free[0] = out_toks[(s, 17)]
            pt_free[1] = out_toks[(s, 17)]
            pSC_free[0] = a2_ln2
            pSC_free[1] = e2
            masks_free = stats_tok
            qrope_free = ("cV", bld.cnt["cV"])

    total_dO = bld.cnt["dO"]
    E("sync", lambda e: e.wait_ge(sems["dO"], total_dO))

    nops = {e: len(bld.ops[e]) for e in ENGINES}

    with nc.Block() as block:
        def make_runner(name):
            def run(e):
                for (ws, fn, inc, n) in bld.ops[name]:
                    for (s2, v) in ws:
                        e.wait_ge(sems[s2], v)
                    ins = fn(e)
                    if inc is not None:
                        ins.then_inc(sems[inc], n)
            return run
        block.tensor(make_runner("tensor"))
        block.scalar(make_runner("scalar"))
        block.vector(make_runner("vector"))
        block.gpsimd(make_runner("gpsimd"))
        block.sync(make_runner("sync"))

    stack.close()
    return nc, nops


def _rope_tables():
    inv = (1.0 / (10000.0 ** (np.arange(0, 64, dtype=np.float32) / np.float32(64.0)))).astype(np.float32)
    pos = np.arange(T, dtype=np.float32)
    ang = inv[:, None] * pos[None, :]
    cos = np.cos(ang).astype(np.float32)
    sin = np.sin(ang).astype(np.float32)
    cc = np.concatenate([cos, cos], axis=0)
    ss = np.concatenate([-sin, sin], axis=0)
    return cc, ss


def _masks_for_role(r):
    out = np.zeros((NSLOT, 8, 128, 512), dtype=np.float32)
    for s in range(NSLOT):
        J = CHUNKS[r][s]
        cpad = PADC[s]
        for jj in range(8):
            kt = cpad - 8 + jj
            if kt < 4 * J:
                out[s, jj] = 1.0
            elif kt < 4 * J + 4:
                q = 512 * J + np.arange(512)[None, :]
                k = 128 * kt + np.arange(128)[:, None]
                out[s, jj] = (q >= k).astype(np.float32)
    return out.astype(ml_dtypes.bfloat16)


def _chunked(w, nch):
    din, n = w.shape
    csz = n // nch
    return np.ascontiguousarray(
        w.reshape(din // 128, 128, nch, csz).transpose(2, 1, 0, 3))


def prep_in_maps(L0_attn_X, L0_attn_Wq, L0_attn_Wk, L0_attn_Wv, L0_attn_Wo,
                 L0_ffn_U, L0_ffn_S, L0_ffn_V):
    X = np.asarray(L0_attn_X, dtype=np.float32)
    wq = _chunked(np.asarray(L0_attn_Wq, dtype=np.float32), H)
    wk = _chunked(np.asarray(L0_attn_Wk, dtype=np.float32), HKV)
    wv = _chunked(np.asarray(L0_attn_Wv, dtype=np.float32), 2)
    wo = _chunked(np.asarray(L0_attn_Wo, dtype=np.float32), KCH)
    u = np.asarray(L0_ffn_U, dtype=np.float32)
    sv = np.asarray(L0_ffn_S, dtype=np.float32)
    vf = _chunked(np.asarray(L0_ffn_V, dtype=np.float32), RCH)
    ut = _chunked(np.ascontiguousarray(u.T), KCH)
    svt = np.ascontiguousarray(sv.reshape(RCH, 128).T)

    cc, ss = _rope_tables()
    masks = [_masks_for_role(0), _masks_for_role(1)]

    in_maps = []
    xts = [np.ascontiguousarray(X[b].T) for b in range(B)]
    xtls = [_chunked(xts[b], NTC) for b in range(B)]
    for cidx in range(8):
        b, r = cidx // 2, cidx % 2
        qcols = np.concatenate([np.arange(512 * J, 512 * (J + 1)) for J in CHUNKS[r]])
        xqt = np.ascontiguousarray(xts[b][:, qcols])
        in_maps.append({
            "xtl": xtls[b], "xql": _chunked(xqt, NSLOT), "xqt": xqt,
            "wq": wq, "wk": wk, "wv": wv, "wo": wo,
            "vf": vf, "ut": ut, "svt": svt,
            "ccin": cc, "ssin": ss,
            "ccq": np.ascontiguousarray(cc[:, qcols]),
            "ssq": np.ascontiguousarray(ss[:, qcols]),
            "maskin": masks[r],
        })
    return in_maps


def kernel(L0_attn_X, L0_attn_Wq, L0_attn_Wk, L0_attn_Wv, L0_attn_Wo,
           L0_ffn_U, L0_ffn_S, L0_ffn_V):
    in_maps = prep_in_maps(L0_attn_X, L0_attn_Wq, L0_attn_Wk, L0_attn_Wv,
                           L0_attn_Wo, L0_ffn_U, L0_ffn_S, L0_ffn_V)
    nc, _ = build_nc()
    res = run_bass_kernel_spmd(nc, in_maps, core_ids=list(range(8)))
    out = np.empty((B, T, D), dtype=np.float32)
    for cidx in range(8):
        ztr = res.results[cidx]["zt"]
        b, r = cidx // 2, cidx % 2
        for s2, J in enumerate(CHUNKS[r]):
            out[b, 512 * J:512 * (J + 1), :] = ztr[:, 512 * s2:512 * (s2 + 1)].T
    return out


if __name__ == "__main__":
    nc, nops = build_nc()
    print("instruction counts per engine:", nops)
    print("sbuf bytes remaining:", nc.sbuf_bytes_remaining)
